# version 02
# baseline (speedup 1.0000x reference)
import numpy as np
import ml_dtypes
from contextlib import ExitStack
import concourse.bass as bass
import concourse.mybir as mybir
from concourse.bass_utils import run_bass_kernel_spmd

F32 = mybir.dt.float32
BF16 = mybir.dt.bfloat16
AF = mybir.ActivationFunctionType
ALU = mybir.AluOpType
AX = mybir.AxisListType
BF = ml_dtypes.bfloat16

D = 1024
DC = 8
TCORE = 2048
NTC = TCORE // 128
EPS_LN = 1e-5


ENGINES = ("tensor", "vector", "scalar", "gpsimd", "sync")

class PB:

    def __init__(self, nc, es):
        self.nc = nc
        self.es = es
        self.sems = {}
        self.cnt = {}
        for s in ENGINES + tuple("d" + e for e in ENGINES) + ("coll",):
            self.sems[s] = es.enter_context(nc.semaphore("S_" + s))
            self.cnt[s] = 0
        self.q = {e: [] for e in ENGINES}
        self.hwm = {}
        self.tail = {}

    def _waits(self, eng, deps, strict):
        need = {}
        for (s, v) in deps:
            if v > need.get(s, 0):
                need[s] = v
        ws = []
        for s, v in need.items():
            k = (eng, s)
            if self.hwm.get(k, 0) >= v:
                continue
            self.hwm[k] = v
            ws.append((s, v))
        return ws

    def emit(self, eng, fn, deps=(), strict=False, dma=False, coll=False):
        ws = self._waits(eng, deps, strict)
        if coll:
            sem = "coll"
        elif dma:
            sem = "d" + eng
        else:
            sem = eng
        self.cnt[sem] += 16 if (dma or coll) else 1
        tok = (sem, self.cnt[sem])
        self.q[eng].append((fn, ws, sem, 16 if (dma or coll) else 1))
        return tok

    def barrier_tok(self, eng):
        return (eng, self.cnt[eng])

    def all_toks(self):
        return [(s, c) for s, c in self.cnt.items() if c > 0]

    def dma(self, eng, out, in_, deps=(), strict=False):
        deps = list(deps) + [("d" + eng, self.cnt["d" + eng])]
        return self.emit(eng, lambda e, o=out, i=in_: e.dma_start(out=o, in_=i),
                         deps=deps, strict=strict, dma=True)

    def build(self):
        block = self.es.enter_context(self.nc.Block())
        for eng in ENGINES:
            items = self.q[eng]
            if not items:
                continue
            def runner(e, items=items):
                for (fn, ws, sem, amt) in items:
                    for (s, v) in ws:
                        e.wait_ge(self.sems[s], v)
                    inst = fn(e)
                    inst.then_inc(self.sems[sem], amt)
            getattr(block, eng)(runner)
        return block


def new_nc():
    return bass.Bass("TRN2", target_bir_lowering=False)



class Buf:
    __slots__ = ("t",)
    def __init__(self, t):
        self.t = t
    def __getitem__(self, idx):
        return self.t[idx]


def sb(pb, name, shape, dtype, side=None):
    return pb.es.enter_context(pb.nc.sbuf_tensor(name, list(shape), dtype, side=side))

def ps(pb, name, shape, dtype=F32):
    return pb.es.enter_context(pb.nc.psum_tensor(name, list(shape), dtype))



def emit_ln_tm(pb, x, xn_bf, stats, sq_scr, n_tc=NTC, tok_x=(), affine_wb=None,
               gamma_bc=None, beta_bc=None, out_extra=None):
    n = n_tc
    toks = []
    t_red = []
    for tc in range(n):
        xs = x[:, tc * D:(tc + 1) * D]
        t1 = pb.emit("scalar", lambda e, xs=xs, tc=tc: e.activation(
            sq_scr[:, 0:D], xs, AF.Square, accum_out=stats[:, n + tc:n + tc + 1]),
            deps=tok_x)
        t2 = pb.emit("vector", lambda e, xs=xs, tc=tc: e.reduce_sum(
            stats[:, tc:tc + 1], xs, axis=AX.X), deps=tok_x)
        t_red += [t1, t2]
    tmu = pb.emit("vector", lambda e: e.tensor_scalar(
        stats[:, 2 * n:3 * n], stats[:, 0:n], 1.0 / D, None, ALU.mult, ALU.bypass),
        deps=t_red, strict=True)
    tm2 = pb.emit("vector", lambda e: e.tensor_tensor(
        stats[:, 0:n], stats[:, 2 * n:3 * n], stats[:, 2 * n:3 * n], ALU.mult),
        deps=[tmu], strict=True)
    tvar = pb.emit("vector", lambda e: e.scalar_tensor_tensor(
        stats[:, n:2 * n], stats[:, n:2 * n], 1.0 / D, stats[:, 0:n],
        ALU.mult, ALU.subtract), deps=[tm2], strict=True)
    tve = pb.emit("vector", lambda e: e.tensor_scalar(
        stats[:, n:2 * n], stats[:, n:2 * n], EPS_LN, None, ALU.add, ALU.bypass),
        deps=[tvar], strict=True)
    tln = pb.emit("scalar", lambda e: e.activation(
        stats[:, n:2 * n], stats[:, n:2 * n], AF.Ln, bias=0.0, scale=1.0),
        deps=[tve], strict=True)
    trstd = pb.emit("scalar", lambda e: e.activation(
        stats[:, 3 * n:4 * n], stats[:, n:2 * n], AF.Exp, bias=0.0, scale=-0.5),
        deps=[tln], strict=True)
    for tc in range(n):
        xs = x[:, tc * D:(tc + 1) * D]
        os = xn_bf[:, tc * D:(tc + 1) * D]
        if affine_wb is None:
            tt = pb.emit("vector", lambda e, xs=xs, os=os, tc=tc: e.tensor_scalar(
                os, xs, stats[:, 2 * n + tc:2 * n + tc + 1],
                stats[:, 3 * n + tc:3 * n + tc + 1], ALU.subtract, ALU.mult),
                deps=[trstd], strict=True)
        else:
            g_bc, b_bc, scr = affine_wb
            t0 = pb.emit("vector", lambda e, xs=xs, tc=tc, scr=scr: e.tensor_scalar(
                scr[:, 0:D], xs, stats[:, 2 * n + tc:2 * n + tc + 1],
                stats[:, 3 * n + tc:3 * n + tc + 1], ALU.subtract, ALU.mult),
                deps=[trstd] + ldep + list(prev_aff), strict=True)
            t0b = pb.emit("vector", lambda e, scr=scr: e.tensor_tensor(
                scr[:, 0:D], scr[:, 0:D], g_bc[:, 0:D], ALU.mult),
                deps=[t0], strict=True)
            tt = pb.emit("vector", lambda e, os=os, scr=scr: e.tensor_tensor(
                os, scr[:, 0:D], b_bc[:, 0:D], ALU.add), deps=[t0b], strict=True)
        toks.append(tt)
    return toks, toks[-1]


def emit_tm_to_fm(pb, src_bf, dst_fm, ident_bf, pt, n_tc=NTC, deps_per_tile=None,
                  src_cols=D, dst_cols=None, extra_deps=()):
    if dst_cols is None:
        dst_cols = n_tc * 128
    ndc = src_cols // 128
    toks = [None] * ndc
    prev_evac = {0: (), 1: ()}
    k = 0
    for tc in range(n_tc):
        dep = list(deps_per_tile[tc]) if deps_per_tile else []
        dep += list(extra_deps)
        for dc in range(ndc):
            half = k % 2
            k += 1
            po = pt[half][:, 0:128]
            ts = pb.emit("tensor", lambda e, tc=tc, dc=dc, po=po: e.transpose(
                po, src_bf[:, tc * src_cols + dc * 128: tc * src_cols + dc * 128 + 128],
                ident_bf[:, 0:128]), deps=dep + list(prev_evac[half]))
            tv = pb.emit("scalar", lambda e, tc=tc, dc=dc, po=po: e.copy(
                dst_fm[:, dc * dst_cols + tc * 128: dc * dst_cols + tc * 128 + 128],
                po), deps=[ts])
            prev_evac[half] = (tv,)
            toks[dc] = tv
    return toks


def emit_weight_load(pb, dram_ap, stage, wbf, nchunks, cols, deps=(), eng_dma="sync",
                     cast_eng="scalar"):
    toks = []
    key = ("stage", stage.name if hasattr(stage, "name") else id(stage))
    prev_cast = pb.tail.get(key)
    for c in range(nchunks):
        sl = stage[:, 0:cols]
        dep = list(deps)
        if prev_cast is not None:
            dep.append(prev_cast)
        td = pb.dma(eng_dma, sl, dram_ap[c], deps=dep)
        prev_cast = pb.emit(cast_eng, lambda e, sl=sl, c=c: e.copy(
            wbf[:, c * cols:(c + 1) * cols], sl), deps=[td])
        toks.append(prev_cast)
    pb.tail[key] = prev_cast
    return toks, toks[-1]



def emit_routing_l0(pb, psum_tok, psum_r, routed_tm, rscr, tc, renorm=True):
    lg = rscr[:, 0:8]
    ex = rscr[:, 8:16]
    rw = rscr[:, 16:24]
    rw2 = rscr[:, 24:32]
    m = rscr[:, 32:33]
    nm = rscr[:, 33:34]
    s = rscr[:, 34:35]
    rs = rscr[:, 35:36]
    m1 = rscr[:, 36:37]
    m2 = rscr[:, 37:38]
    a = rscr[:, 38:39]
    d1 = rscr[:, 39:40]
    s2 = rscr[:, 40:41]
    d2 = rscr[:, 41:42]
    den = rscr[:, 42:43]
    coef = rscr[:, 43:44]
    mask1 = rscr[:, 44:52]
    mask2 = rscr[:, 52:60]
    tl = pb.emit("vector", lambda e: e.tensor_copy(lg, psum_r[:, 0:8]), deps=[psum_tok])
    t1 = pb.emit("vector", lambda e: e.reduce_max(m, lg, axis=AX.X), deps=[tl], strict=True)
    t2 = pb.emit("vector", lambda e: e.tensor_scalar(nm, m, -1.0, None, ALU.mult, ALU.bypass), deps=[t1], strict=True)
    t3 = pb.emit("scalar", lambda e: e.activation(ex, lg, AF.Exp, bias=nm[:, 0:1], scale=1.0), deps=[t2], strict=True)
    t4 = pb.emit("vector", lambda e: e.reduce_sum(s, ex, axis=AX.X), deps=[t3], strict=True)
    t5 = pb.emit("vector", lambda e: e.reciprocal(rs, s), deps=[t4], strict=True)
    t6 = pb.emit("vector", lambda e: e.tensor_scalar(rw, ex, rs[:, 0:1], None, ALU.mult, ALU.bypass), deps=[t5], strict=True)
    t7 = pb.emit("vector", lambda e: e.reduce_max(m1, rw, axis=AX.X), deps=[t6], strict=True)
    t8 = pb.emit("vector", lambda e: e.tensor_scalar(mask1, rw, m1[:, 0:1], None, ALU.is_ge, ALU.bypass), deps=[t7], strict=True)
    t9 = pb.emit("vector", lambda e: e.scalar_tensor_tensor(rw2, mask1, -1.0, rw, ALU.mult, ALU.add), deps=[t8], strict=True)
    t10 = pb.emit("vector", lambda e: e.reduce_max(m2, rw2, axis=AX.X), deps=[t9], strict=True)
    t11 = pb.emit("vector", lambda e: e.tensor_scalar(mask2, rw2, m2[:, 0:1], None, ALU.is_ge, ALU.bypass), deps=[t10], strict=True)
    if renorm:
        t12 = pb.emit("vector", lambda e: e.tensor_tensor(a, m1, m2, ALU.add), deps=[t11], strict=True)
        t13 = pb.emit("vector", lambda e: e.tensor_scalar(d1, a, 1e-9, None, ALU.add, ALU.bypass), deps=[t12], strict=True)
        rd1 = rscr[:, 60:61]
        t13b = pb.emit("vector", lambda e: e.reciprocal(rd1, d1), deps=[t13], strict=True)
        t14 = pb.emit("vector", lambda e: e.tensor_tensor(s2, a, rd1, ALU.mult), deps=[t13b], strict=True)
        t15 = pb.emit("vector", lambda e: e.tensor_scalar(d2, s2, 1e-9, None, ALU.add, ALU.bypass), deps=[t14], strict=True)
        t16 = pb.emit("vector", lambda e: e.tensor_tensor(den, d1, d2, ALU.mult), deps=[t15], strict=True)
        t17 = pb.emit("vector", lambda e: e.reciprocal(coef, den), deps=[t16], strict=True)
    else:
        t17 = pb.emit("vector", lambda e: e.memset(coef, 1.0), deps=[t11], strict=True)
    ta = pb.emit("vector", lambda e: e.tensor_scalar(mask1, mask1, m1[:, 0:1], coef[:, 0:1], ALU.mult, ALU.mult), deps=[t17], strict=True)
    tb = pb.emit("vector", lambda e: e.tensor_scalar(mask2, mask2, m2[:, 0:1], coef[:, 0:1], ALU.mult, ALU.mult), deps=[ta], strict=True)
    td = pb.emit("vector", lambda e: e.tensor_tensor(
        routed_tm[:, tc * 8:(tc + 1) * 8], mask1, mask2, ALU.add), deps=[tb], strict=True)
    return td




def emit_moe_l0(pb, bufs, wap, n_tc=NTC, fm_toks=None, renorm=True,
                shared=False, l1=False):
    x = bufs["x"]; xn = bufs["xn_fm"]
    TT_ = n_tc * 128
    fm_dep = ([fm_toks[-1]] if fm_toks else [])
    pr = bufs["ps_r"]
    prev_route = None
    for tc in range(n_tc):
        mm_toks = []
        for dc in range(DC):
            dep = list(fm_dep)
            if prev_route is not None and dc == 0:
                dep.append(prev_route)
            tmm = pb.emit("tensor", lambda e, tc=tc, dc=dc: e.matmul(
                pr[:, 0:8], xn[:, dc * TT_ + tc * 128: dc * TT_ + tc * 128 + 128],
                bufs["gwT"][:, dc * 8:(dc + 1) * 8],
                start=(dc == 0), stop=(dc == DC - 1)), deps=dep)
            mm_toks.append(tmm)
        if l1:
            prev_route = emit_routing_l1(pb, mm_toks[-1], pr, bufs["routed_tm"],
                                         bufs["rscr"], tc, bufs["bias_bc"],
                                         bias_tok=bufs.get("bias_tok", ("sync", 0)))
        else:
            prev_route = emit_routing_l0(pb, mm_toks[-1], pr, bufs["routed_tm"],
                                         bufs["rscr"], tc, renorm=renorm)
    route_done = prev_route

    n_exp = 8
    gu_bufs = [bufs["gu_bf0"], bufs["gu_bf1"]]
    dn_bufs = [bufs["dn_bf0"], bufs.get("dn_bf1", bufs["dn_bf0"])]
    h_bufs = [bufs["hbuf0"], bufs["hbuf1"]]
    w_free = [(), ()]
    h_free = {0: (), 1: ()}
    x_tok = {}
    psg = bufs["ps_g"]; psu = bufs["ps_u"]; psd = bufs["ps_d"]
    prev_g_use = ()
    prev_u_use = ()
    prev_sg_use = ()
    prev_d_use = {0: (), 1: ()}
    out_toks = []
    for e_i in range(n_exp + (1 if shared else 0)):
        is_shared = (e_i == n_exp)
        half = e_i % 2
        gub = gu_bufs[half]; dnb = dn_bufs[half]
        gu_src = wap["sgu"] if is_shared else wap["gu"][e_i]
        dn_src = wap["sdown"] if is_shared else wap["down"][e_i]
        _, tgu = emit_weight_load(pb, gu_src, bufs["stage"], gub, len(gu_src), 1024,
                                  deps=w_free[half], eng_dma="sync")
        _, tdn = emit_weight_load(pb, dn_src, bufs["stage2"], dnb, len(dn_src), 1024,
                                  deps=list(w_free[0]) + list(w_free[1]), eng_dma="gpsimd")
        last_use = []
        for tt in range(n_tc // 4):
            hh = tt % 2
            hb = h_bufs[hh]
            h_ready = []
            for hf in range(8):
                gmm = umm = None
                for dc in range(DC):
                    dep = [tgu] + (list(prev_g_use) if dc == 0 else []) + fm_dep
                    gmm = pb.emit("tensor", lambda e, gub=gub, tt=tt, hf=hf, dc=dc: e.matmul(
                        psg[:, 0:512],
                        gub[:, dc * 2048 + hf * 128: dc * 2048 + hf * 128 + 128],
                        xn[:, dc * TT_ + tt * 512: dc * TT_ + tt * 512 + 512],
                        start=(dc == 0), stop=(dc == DC - 1)), deps=dep)
                for dc in range(DC):
                    dep = [tgu] + (list(prev_u_use) if dc == 0 else []) + fm_dep
                    umm = pb.emit("tensor", lambda e, gub=gub, tt=tt, hf=hf, dc=dc: e.matmul(
                        psu[:, 0:512],
                        gub[:, dc * 2048 + 1024 + hf * 128: dc * 2048 + 1024 + hf * 128 + 128],
                        xn[:, dc * TT_ + tt * 512: dc * TT_ + tt * 512 + 512],
                        start=(dc == 0), stop=(dc == DC - 1)), deps=dep)
                tsg = pb.emit("scalar", lambda e: e.activation(
                    bufs["sg"][:, 0:512], psg[:, 0:512], AF.Silu),
                    deps=[gmm] + list(prev_sg_use))
                th = pb.emit("vector", lambda e, hb=hb, hf=hf: e.tensor_tensor(
                    hb[:, hf * 512:(hf + 1) * 512], bufs["sg"][:, 0:512],
                    psu[:, 0:512], ALU.mult),
                    deps=[tsg, umm] + list(h_free[hh]), strict=True)
                prev_g_use = (tsg,)
                prev_u_use = (th,)
                prev_sg_use = (th,)
                h_ready.append(th)
            d_toks = []
            for tc2 in range(4):
                tcg = tt * 4 + tc2
                for dc2 in range(2):
                    pd = psd[:, dc2 * 512:(dc2 + 1) * 512]
                    dmm = None
                    for hf in range(8):
                        dep = [h_ready[hf], tdn]
                        if hf == 0:
                            dep = dep + list(prev_d_use[dc2])
                        dmm = pb.emit("tensor", lambda e, hb=hb, dnb=dnb, tc2=tc2, dc2=dc2, hf=hf, pd=pd: e.matmul(
                            pd,
                            hb[:, hf * 512 + tc2 * 128: hf * 512 + tc2 * 128 + 128],
                            dnb[:, hf * 1024 + dc2 * 512: hf * 1024 + dc2 * 512 + 512],
                            start=(hf == 0), stop=(hf == 7)), deps=dep)
                    xs = x[:, tcg * D + dc2 * 512: tcg * D + dc2 * 512 + 512]
                    xdep = [dmm, x_tok.get((tcg, dc2), ("vector", 0))]
                    if is_shared:
                        tacc = pb.emit("vector", lambda e, xs=xs, pd=pd: e.tensor_tensor(
                            xs, pd, xs, ALU.add), deps=xdep, strict=True)
                    else:
                        rcol = bufs["routed_tm"][:, tcg * 8 + e_i: tcg * 8 + e_i + 1]
                        tacc = pb.emit("vector", lambda e, xs=xs, pd=pd, rcol=rcol: e.scalar_tensor_tensor(
                            xs, pd, rcol, xs, ALU.mult, ALU.add),
                            deps=xdep + [route_done], strict=True)
                    x_tok[(tcg, dc2)] = tacc
                    prev_d_use[dc2] = (tacc,)
                    d_toks.append(tacc)
                    last_use.append(tacc)
            h_free[hh] = (d_toks[-1],)
        w_free[half] = (last_use[-1],)
    return [x_tok[k] for k in x_tok]


def emit_ln_fm(pb, x, xn_fm, stats, sq_scr, tmp_bf, ident_bf, pt, n_tc=NTC,
               tok_x=(), affine_wb=None, n_dc=DC, dst_off=0, dst_stride=None,
               src_dram=None, dram_scr=None, dma_eng="gpsimd"):
    n = n_tc
    t_red = []
    prev_sq = None
    scr_free = {0: (), 1: ()}
    def tile_src(tc):
        half = tc % 2
        sl = dram_scr[:, half * D:(half + 1) * D]
        dep = list(tok_x) + list(scr_free[half])
        t = pb.dma(dma_eng, sl, src_dram[tc], deps=dep)
        return sl, [t], half
    for tc in range(n):
        if src_dram is None:
            xs = x[:, tc * D:(tc + 1) * D]
            dep0 = list(tok_x)
            half = None
        else:
            xs, dep0, half = tile_src(tc)
        dep1 = dep0 + ([prev_sq] if prev_sq is not None else [])
        t1 = pb.emit("scalar", lambda e, xs=xs, tc=tc: e.activation(
            sq_scr[:, 0:D], xs, AF.Square, accum_out=stats[:, n + tc:n + tc + 1]),
            deps=dep1, strict=True)
        prev_sq = t1
        t2 = pb.emit("vector", lambda e, xs=xs, tc=tc: e.reduce_sum(
            stats[:, tc:tc + 1], xs, axis=AX.X), deps=dep0)
        t_red += [t1, t2]
        if src_dram is not None:
            scr_free[half] = (t1, t2)
    tmu = pb.emit("vector", lambda e: e.tensor_scalar(
        stats[:, 2 * n:3 * n], stats[:, 0:n], 1.0 / D, None, ALU.mult, ALU.bypass),
        deps=t_red, strict=True)
    tm2 = pb.emit("vector", lambda e: e.tensor_tensor(
        stats[:, 0:n], stats[:, 2 * n:3 * n], stats[:, 2 * n:3 * n], ALU.mult),
        deps=[tmu], strict=True)
    tvar = pb.emit("vector", lambda e: e.scalar_tensor_tensor(
        stats[:, n:2 * n], stats[:, n:2 * n], 1.0 / D, stats[:, 0:n],
        ALU.mult, ALU.subtract), deps=[tm2], strict=True)
    tve = pb.emit("vector", lambda e: e.tensor_scalar(
        stats[:, n:2 * n], stats[:, n:2 * n], EPS_LN, None, ALU.add, ALU.bypass),
        deps=[tvar], strict=True)
    tln = pb.emit("scalar", lambda e: e.activation(
        stats[:, n:2 * n], stats[:, n:2 * n], AF.Ln, bias=0.0, scale=1.0),
        deps=[tve], strict=True)
    trstd = pb.emit("scalar", lambda e: e.activation(
        stats[:, 3 * n:4 * n], stats[:, n:2 * n], AF.Exp, bias=0.0, scale=-0.5),
        deps=[tln], strict=True)
    TT_ = dst_stride if dst_stride is not None else n * 128
    nh = tmp_bf.shape[1] // D if hasattr(tmp_bf, "shape") else 1
    nh = max(1, nh)
    prev_tr = {i: () for i in range(nh)}
    prev_pt = {0: (), 1: ()}
    prev_aff = ()
    last = None
    k = 0
    for tc in range(n):
        half = tc % nh
        if src_dram is None:
            xs = x[:, tc * D:(tc + 1) * D]
            ldep = []
            shalf = None
        else:
            xs, ldep, shalf = tile_src(tc)
        os = tmp_bf[:, half * D:(half + 1) * D]
        if affine_wb is None:
            tt = pb.emit("vector", lambda e, xs=xs, os=os, tc=tc: e.tensor_scalar(
                os, xs, stats[:, 2 * n + tc:2 * n + tc + 1],
                stats[:, 3 * n + tc:3 * n + tc + 1], ALU.subtract, ALU.mult),
                deps=[trstd] + list(prev_tr[half]) + ldep, strict=True)
            if shalf is not None:
                scr_free[shalf] = (tt,)
        else:
            g_bc, b_bc, scr = affine_wb
            t0 = pb.emit("vector", lambda e, xs=xs, tc=tc, scr=scr: e.tensor_scalar(
                scr[:, 0:D], xs, stats[:, 2 * n + tc:2 * n + tc + 1],
                stats[:, 3 * n + tc:3 * n + tc + 1], ALU.subtract, ALU.mult),
                deps=[trstd] + ldep + list(prev_aff), strict=True)
            t0b = pb.emit("vector", lambda e, scr=scr: e.tensor_tensor(
                scr[:, 0:D], scr[:, 0:D], g_bc[:, 0:D], ALU.mult),
                deps=[t0], strict=True)
            tt = pb.emit("vector", lambda e, os=os, scr=scr: e.tensor_tensor(
                os, scr[:, 0:D], b_bc[:, 0:D], ALU.add),
                deps=[t0b] + list(prev_tr[half]), strict=True)
            prev_aff = (tt,)
            if shalf is not None:
                scr_free[shalf] = (tt,)
        trs = []
        for dc in range(n_dc):
            ph = k % 2
            k += 1
            po = pt[ph][:, 0:128]
            ts = pb.emit("tensor", lambda e, os=os, dc=dc, po=po: e.transpose(
                po, os[:, dc * 128:dc * 128 + 128], ident_bf[:, 0:128]),
                deps=[tt] + list(prev_pt[ph]))
            tv = pb.emit("scalar", lambda e, tc=tc, dc=dc, po=po: e.copy(
                xn_fm[:, dc * TT_ + dst_off + tc * 128: dc * TT_ + dst_off + tc * 128 + 128], po),
                deps=[ts])
            prev_pt[ph] = (tv,)
            trs.append(tv)
            last = tv
        prev_tr[half] = (trs[-1],)
    return last



def chunk_w(w):
    w = np.ascontiguousarray(np.asarray(w, np.float32))
    return w.reshape(w.shape[0] // 128, 128, w.shape[1])

IDENT_BF = np.eye(128, dtype=BF)


def build_moe_l0_test(n_tc=NTC):
    nc = new_nc()
    es = ExitStack()
    T = n_tc * 128
    xin = nc.dram_tensor("xin", [n_tc, 128, D], F32, kind="ExternalInput").ap()
    gwT = nc.dram_tensor("gwT", [128, DC * 8], F32, kind="ExternalInput").ap()
    gu = nc.dram_tensor("gu", [8, DC, 128, 2048], F32, kind="ExternalInput").ap()
    down = nc.dram_tensor("down", [8, DC, 128, 1024], F32, kind="ExternalInput").ap()
    ident = nc.dram_tensor("ident", [128, 128], BF16, kind="ExternalInput").ap()
    xout = nc.dram_tensor("xout", [n_tc, 128, D], F32, kind="ExternalOutput").ap()
    with es:
        pb = PB(nc, es)
        bufs = dict(
            x=sb(pb, "x", [128, n_tc * D], F32),
            xn_fm=sb(pb, "xn_fm", [128, DC * T], BF16),
            tmp_bf=sb(pb, "tmp_bf", [128, D], BF16),
            stats=sb(pb, "stats", [128, 4 * n_tc], F32),
            identb=sb(pb, "identb", [128, 128], BF16),
            gwT=sb(pb, "gwTb", [128, DC * 8], BF16),
            stage=sb(pb, "stage", [128, 1024], F32),
            stage2=sb(pb, "stage2", [128, 1024], F32),
            gu_bf0=sb(pb, "gu_bf0", [128, DC * 2048], BF16),
            gu_bf1=sb(pb, "gu_bf1", [128, DC * 2048], BF16),
            dn_bf0=sb(pb, "dn_bf0", [128, DC * 1024], BF16),
            hbuf0=sb(pb, "hbuf0", [128, 4096], BF16),
            hbuf1=sb(pb, "hbuf1", [128, 4096], BF16),
            sg=sb(pb, "sg", [128, 512], F32),
            routed_tm=sb(pb, "routed_tm", [128, n_tc * 8], F32),
            rscr=sb(pb, "rscr", [128, 64], F32),
            ps_r=ps(pb, "ps_r", [128, 8]),
            ps_g=ps(pb, "ps_g", [128, 512]),
            ps_u=ps(pb, "ps_u", [128, 512]),
            ps_d=ps(pb, "ps_d", [128, 1024]),
            pt=(ps(pb, "pt0", [128, 128], BF16), ps(pb, "pt1", [128, 128], BF16)),
        )
        tid = pb.dma("sync", bufs["identb"][:, :], ident)
        x_toks = []
        for g in range(0, n_tc, 4):
            gn = min(4, n_tc - g)
            x_toks.append(pb.dma(
                "sync",
                bufs["x"][:, g * D:(g + gn) * D].rearrange("p (t d) -> p t d", t=gn),
                xin[g:g + gn].rearrange("t p d -> p t d")))
        tgw = pb.dma("sync", bufs["rscr"][:, 0:64], gwT)
        tgwb = pb.emit("vector", lambda e: e.tensor_copy(bufs["gwT"][:, :], bufs["rscr"][:, 0:64]), deps=[tgw])
        fm_last = emit_ln_fm(pb, bufs["x"], bufs["xn_fm"], bufs["stats"],
                             bufs["tmp_bf"][:, 0:D],
                             bufs["tmp_bf"], bufs["identb"], bufs["pt"], n_tc=n_tc,
                             tok_x=x_toks + [tid, tgwb])
        class W:
            pass
        wap = {
            "gu": [[gu[e_i, c // 2][:, (c % 2) * 1024:(c % 2) * 1024 + 1024] for c in range(DC * 2)] for e_i in range(8)],
            "down": [[down[e_i, c] for c in range(DC)] for e_i in range(8)],
        }
        moe_toks = emit_moe_l0(pb, bufs, wap, n_tc=n_tc, fm_toks=[fm_last], renorm=True)
        st_toks = []
        for g in range(0, n_tc, 4):
            gn = min(4, n_tc - g)
            st_toks.append(pb.dma(
                "gpsimd", xout[g:g + gn].rearrange("t p d -> p t d"),
                bufs["x"][:, g * D:(g + gn) * D].rearrange("p (t d) -> p t d", t=gn),
                deps=moe_toks))
        pb.emit("sync", lambda e: e.nop(), deps=st_toks)
        pb.build()
    return nc



def emit_routing_l1(pb, psum_tok, psum_r, routed_tm, rscr, tc, bias_bc, bias_tok=("sync", 0)):
    lg = rscr[:, 0:8]
    ex = rscr[:, 8:16]
    rw = rscr[:, 16:24]
    sp = rscr[:, 24:32]
    m = rscr[:, 32:33]
    nm = rscr[:, 33:34]
    s = rscr[:, 34:35]
    rs = rscr[:, 35:36]
    m1 = rscr[:, 36:37]
    m2 = rscr[:, 37:38]
    mask1 = rscr[:, 44:52]
    mask2 = rscr[:, 52:60]
    sp2 = rscr[:, 24:32]
    tl = pb.emit("vector", lambda e: e.tensor_copy(lg, psum_r[:, 0:8]), deps=[psum_tok])
    t1 = pb.emit("vector", lambda e: e.reduce_max(m, lg, axis=AX.X), deps=[tl], strict=True)
    t2 = pb.emit("vector", lambda e: e.tensor_scalar(nm, m, -1.0, None, ALU.mult, ALU.bypass), deps=[t1], strict=True)
    t3 = pb.emit("scalar", lambda e: e.activation(ex, lg, AF.Exp, bias=nm[:, 0:1], scale=1.0), deps=[t2], strict=True)
    t4 = pb.emit("vector", lambda e: e.reduce_sum(s, ex, axis=AX.X), deps=[t3], strict=True)
    t5 = pb.emit("vector", lambda e: e.reciprocal(rs, s), deps=[t4], strict=True)
    t6 = pb.emit("vector", lambda e: e.tensor_scalar(rw, ex, rs[:, 0:1], None, ALU.mult, ALU.bypass), deps=[t5], strict=True)
    t7 = pb.emit("vector", lambda e: e.tensor_tensor(sp, rw, bias_bc[:, 0:8], ALU.add), deps=[t6, bias_tok], strict=True)
    t8 = pb.emit("vector", lambda e: e.reduce_max(m1, sp, axis=AX.X), deps=[t7], strict=True)
    t9 = pb.emit("vector", lambda e: e.tensor_scalar(mask1, sp, m1[:, 0:1], None, ALU.is_ge, ALU.bypass), deps=[t8], strict=True)
    t10 = pb.emit("vector", lambda e: e.scalar_tensor_tensor(sp2, mask1, -1e30, sp, ALU.mult, ALU.add), deps=[t9], strict=True)
    t11 = pb.emit("vector", lambda e: e.reduce_max(m2, sp2, axis=AX.X), deps=[t10], strict=True)
    t12 = pb.emit("vector", lambda e: e.tensor_scalar(mask2, sp2, m2[:, 0:1], None, ALU.is_ge, ALU.bypass), deps=[t11], strict=True)
    t13 = pb.emit("vector", lambda e: e.tensor_tensor(mask1, mask1, mask2, ALU.add), deps=[t12], strict=True)
    td = pb.emit("vector", lambda e: e.tensor_tensor(
        routed_tm[:, tc * 8:(tc + 1) * 8], rw, mask1, ALU.mult), deps=[t13], strict=True)
    return td



def emit_weight_load_off(pb, chunks, stage, wbf, cols, col_off, deps=(), eng_dma="sync",
                         cast_eng="scalar"):
    key = ("stage", stage.name if hasattr(stage, "name") else id(stage))
    prev_cast = pb.tail.get(key)
    for c, ap in enumerate(chunks):
        sl = stage[:, 0:cols]
        dep = list(deps)
        if prev_cast is not None:
            dep.append(prev_cast)
        td = pb.dma(eng_dma, sl, ap, deps=dep)
        prev_cast = pb.emit(cast_eng, lambda e, sl=sl, c=c: e.copy(
            wbf[:, col_off + c * cols: col_off + (c + 1) * cols], sl), deps=[td])
    pb.tail[key] = prev_cast
    return prev_cast


def emit_proj_residual(pb, bufs, src_fm, w_bf, n_tc, out_x, act=None, bias_row=None,
                       src_tok=(), w_tok=(), x_tok=None, scale_cols=None):
    T = n_tc * 128
    pd = bufs["ps_q"]
    scr = bufs.get("projscr")
    prev_use = pb.tail.get(("ps_proj",))
    scr_use = {}
    toks = []
    for tc in range(n_tc):
        for dc2 in range(2):
            mm = None
            for hf in range(DC):
                dep = list(src_tok) + list(w_tok) + ([prev_use] if (hf == 0 and prev_use is not None) else [])
                if hf == 0 and prev_use is not None:
                    dep = dep + [prev_use]
                mm = pb.emit("tensor", lambda e, tc=tc, dc2=dc2, hf=hf: e.matmul(
                    pd[:, 0:512],
                    src_fm[:, hf * T + tc * 128: hf * T + tc * 128 + 128],
                    w_bf[:, hf * 1024 + dc2 * 512: hf * 1024 + dc2 * 512 + 512],
                    start=(hf == 0), stop=(bias_row is None and hf == DC - 1)),
                    deps=dep)
            if bias_row is not None:
                mm = pb.emit("tensor", lambda e, dc2=dc2: e.matmul(
                    pd[:, 0:512], bufs["onescol"][0:1, 0:128],
                    bias_row[0:1, dc2 * 512:dc2 * 512 + 512],
                    start=False, stop=True), deps=[])
            xs = out_x[:, tc * D + dc2 * 512: tc * D + dc2 * 512 + 512]
            xdep = [mm]
            if x_tok is not None:
                xdep.append(x_tok.get((tc, dc2), ("vector", 0)))
            if act is not None:
                ta = pb.emit("scalar", lambda e, dc2=dc2: e.activation(
                    scr[:, dc2 * 512:dc2 * 512 + 512], pd[:, 0:512], act),
                    deps=[mm] + scr_use.get(dc2, []))
                tacc = pb.emit("vector", lambda e, xs=xs, dc2=dc2: e.tensor_tensor(
                    xs, scr[:, dc2 * 512:dc2 * 512 + 512], xs, ALU.add),
                    deps=[ta] + xdep[1:], strict=True)
                scr_use[dc2] = [tacc]
                prev_use = ta
            else:
                tacc = pb.emit("vector", lambda e, xs=xs: e.tensor_tensor(
                    xs, pd[:, 0:512], xs, ALU.add), deps=xdep, strict=True)
                prev_use = tacc
            if x_tok is not None:
                x_tok[(tc, dc2)] = tacc
            toks.append(tacc)
    pb.tail[("ps_proj",)] = prev_use
    return toks


def emit_attn(pb, bufs, cfg):
    nq = cfg["nq_tc"]; nkv = 2 * nq
    T = nq * 128; KVT = nkv * 128
    H = cfg["heads"]; dh = cfg["dh"]
    xkv = bufs["xkv_fm"]
    tanh = cfg.get("tanh", False)
    alibi = cfg.get("alibi", False)
    lora = cfg.get("lora", False)
    AFp = AF.Tanh if tanh else AF.Copy
    ps_q = bufs["ps_q"]
    ps_s = [bufs["ps_s0"], bufs["ps_s1"]]
    ps_o = bufs["ps_o"]
    pbuf = [bufs["pb0"], bufs["pb1"]]
    ctx = bufs["ctx_tm"]
    HDall = H * dh
    xkv_tok = bufs["xkv_tok"]
    const_tok = bufs["const_tok"]

    psq_use = None
    t1_tok = ()
    if lora:
        t1 = {n: bufs["t1_" + n] for n in ("q", "k", "v")}
        last = None
        for name in ("q", "k", "v"):
            bT = bufs["bT_" + name]
            for tt in range(KVT // 512):
                mm = None
                for dc in range(DC):
                    dep = ([xkv_tok, const_tok] + list(bufs.get("lw_tok", ())) +
                           ([psq_use] if (psq_use and dc == 0) else []))
                    mm = pb.emit("tensor", lambda e, name=name, tt=tt, dc=dc, bT=bT: e.matmul(
                        ps_q[0:8, 0:512], bT[:, dc * 8:(dc + 1) * 8],
                        xkv[:, dc * KVT + tt * 512: dc * KVT + tt * 512 + 512],
                        start=(dc == 0), stop=(dc == DC - 1)), deps=dep)
                last = pb.emit("scalar", lambda e, name=name, tt=tt: e.activation(
                    t1[name][0:8, tt * 512:tt * 512 + 512], ps_q[0:8, 0:512],
                    AF.Copy, bias=0.0, scale=2.0), deps=[mm])
                psq_use = last
        t1_tok = (last,)

    qh = [bufs["qh0"], bufs["qh1"]]
    kh = [bufs["kh0"], bufs["kh1"]]
    vh = [bufs["vh0"], bufs["vh1"]]
    wstage = bufs["wstage"]
    whb = [bufs["wh0"], bufs["wh1"]]
    biasc = [bufs["bias0"], bufs["bias1"]]
    head_free = [(), ()]
    pexp = [None, None]
    ppv = [(), ()]
    pso_use = ()
    x_toks = []
    for h in range(H):
        hf = h % 2
        Q, K, V, W = qh[hf], kh[hf], vh[hf], whb[hf]
        wt = {}
        for wi, name in enumerate(("wq", "wk", "wv")):
            wt[name] = emit_weight_load_off(
                pb, cfg[name][h], wstage, W, 64, wi * 512,
                deps=head_free[hf], eng_dma="sync")
        if alibi:
            tb = pb.emit("vector", lambda e, h=h, hf=hf: e.scalar_tensor_tensor(
                biasc[hf][:, 0:nkv], bufs["kpos"][:, 0:nkv], -cfg["slopes"][h],
                bufs["kmask"][:, 0:nkv], ALU.mult, ALU.add),
                deps=[const_tok] + list(head_free[hf]))
        else:
            tb = pb.emit("vector", lambda e, hf=hf: e.tensor_copy(
                biasc[hf][:, 0:nkv], bufs["kmask"][:, 0:nkv]),
                deps=[const_tok] + list(head_free[hf]))
        q_toks = []
        k_toks = []
        v_toks = []
        for tt in range(T // 512):
            mm = None
            for dc in range(DC):
                dep = ([wt["wq"], xkv_tok] + list(head_free[hf]) +
                       ([psq_use] if (psq_use and dc == 0) else []))
                mm = pb.emit("tensor", lambda e, W=W, tt=tt, dc=dc: e.matmul(
                    ps_q[0:64, 0:512], W[:, dc * 64: dc * 64 + 64],
                    xkv[:, dc * KVT + KVT // 2 + tt * 512: dc * KVT + KVT // 2 + tt * 512 + 512],
                    start=(dc == 0), stop=(not lora and dc == DC - 1)), deps=dep)
            if lora:
                mm = pb.emit("tensor", lambda e, h=h, tt=tt: e.matmul(
                    ps_q[0:64, 0:512], bufs["aT_q"][0:8, h * 64:(h + 1) * 64],
                    bufs["t1_q"][0:8, KVT // 2 + tt * 512: KVT // 2 + tt * 512 + 512],
                    start=False, stop=True), deps=list(t1_tok) + list(bufs.get("lw_tok", ())))
            tq = pb.emit("scalar", lambda e, Q=Q, tt=tt, h=h: e.activation(
                Q[0:64, tt * 512:tt * 512 + 512], ps_q[0:64, 0:512], AFp,
                bias=(bufs["bqk"][0:64, 2 * h:2 * h + 1] if tanh else 0.0),
                scale=1.0), deps=[mm, bufs.get("bqk_tok", ("scalar", 0))], strict=True)
            psq_use = tq
            q_toks.append(tq)
        for tt in range(KVT // 512):
            mm = None
            for dc in range(DC):
                dep = [wt["wk"], xkv_tok] + list(head_free[hf]) + ([psq_use] if dc == 0 else [])
                mm = pb.emit("tensor", lambda e, W=W, tt=tt, dc=dc: e.matmul(
                    ps_q[0:64, 0:512], W[:, 512 + dc * 64: 512 + dc * 64 + 64],
                    xkv[:, dc * KVT + tt * 512: dc * KVT + tt * 512 + 512],
                    start=(dc == 0), stop=(not lora and dc == DC - 1)), deps=dep)
            if lora:
                mm = pb.emit("tensor", lambda e, h=h, tt=tt: e.matmul(
                    ps_q[0:64, 0:512], bufs["aT_k"][0:8, h * 64:(h + 1) * 64],
                    bufs["t1_k"][0:8, tt * 512: tt * 512 + 512],
                    start=False, stop=True), deps=list(t1_tok))
            tk = pb.emit("scalar", lambda e, K=K, tt=tt, h=h: e.activation(
                K[0:64, tt * 512:tt * 512 + 512], ps_q[0:64, 0:512], AFp,
                bias=(bufs["bqk"][0:64, 2 * h + 1:2 * h + 2] if tanh else 0.0),
                scale=1.0), deps=[mm, bufs.get("bqk_tok", ("scalar", 0))], strict=True)
            psq_use = tk
            k_toks.append(tk)
        tmset = pb.emit("vector", lambda e, V=V: e.memset(V[:, :], 1.0),
                        deps=list(head_free[hf]))
        for kc in range(nkv):
            mm = None
            for dc in range(DC):
                dep = [wt["wv"], xkv_tok, tmset] + ([psq_use] if dc == 0 else [])
                mm = pb.emit("tensor", lambda e, W=W, kc=kc, dc=dc: e.matmul(
                    ps_q[0:128, 0:64],
                    xkv[:, dc * KVT + kc * 128: dc * KVT + kc * 128 + 128],
                    W[:, 1024 + dc * 64: 1024 + dc * 64 + 64],
                    start=(dc == 0), stop=(not lora and not tanh and dc == DC - 1)), deps=dep)
            if lora:
                mm = pb.emit("tensor", lambda e, h=h, kc=kc: e.matmul(
                    ps_q[0:128, 0:64], bufs["t1_v"][0:8, kc * 128: kc * 128 + 128],
                    bufs["aT_v"][0:8, h * 64:(h + 1) * 64],
                    start=False, stop=(not tanh)), deps=list(t1_tok))
            if tanh:
                mm = pb.emit("tensor", lambda e, h=h: e.matmul(
                    ps_q[0:128, 0:64], bufs["onescol"][0:1, 0:128],
                    bufs["bv_rows"][0:1, h * 64:(h + 1) * 64],
                    start=False, stop=True), deps=[const_tok])
            elif not lora:
                pass
            tv = pb.emit("scalar", lambda e, V=V, kc=kc: e.activation(
                V[:, kc * 65: kc * 65 + 64], ps_q[0:128, 0:64], AFp), deps=[mm, tmset],
                strict=True)
            psq_use = tv
            v_toks.append(tv)
        qkv_tok = q_toks[-1]
        kv_done = [k_toks[-1], v_toks[-1], qkv_tok, tb]
        last_s = ()
        last_pv = ()
        for qt in range(T // 512):
            nkc = nq + 4 * qt + 4
            pend = []
            evd = []
            for kc in range(nkc):
                sl = kc % 2
                dep = [pexp[sl]] if pexp[sl] else []
                ts = pb.emit("tensor", lambda e, K=K, Q=Q, kc=kc, qt=qt, sl=sl: e.matmul(
                    ps_s[sl][:, 0:512], K[0:64, kc * 128: kc * 128 + 128],
                    Q[0:64, qt * 512: qt * 512 + 512],
                    start=True, stop=True), deps=dep + kv_done)
                band = kc - (nq + 4 * qt)
                tm_ = ts
                if band >= 0:
                    tm_ = pb.emit("vector", lambda e, sl=sl, band=band: e.tensor_tensor(
                        ps_s[sl][:, 0:512], ps_s[sl][:, 0:512],
                        bufs["cmask"][:, band * 512:(band + 1) * 512], ALU.add),
                        deps=[ts], strict=True)
                te = pb.emit("scalar", lambda e, sl=sl, kc=kc, hf=hf: e.activation(
                    pbuf[sl][:, 0:512], ps_s[sl][:, 0:512], AF.Exp,
                    bias=biasc[hf][:, kc:kc + 1], scale=cfg.get("sscale", 0.125)),
                    deps=[tm_, tb] + list(ppv[sl]), strict=True)
                pexp[sl] = te
                for qs in range(4):
                    dep = [te]
                    if kc == 0:
                        dep += list(pso_use)
                    tpv = pb.emit("tensor", lambda e, V=V, sl=sl, qs=qs, kc=kc: e.matmul(
                        ps_o[qs][:, 0:65],
                        pbuf[sl][:, qs * 128:(qs + 1) * 128],
                        V[:, kc * 65: kc * 65 + 65],
                        start=(kc == 0), stop=(kc == nkc - 1)), deps=dep)
                    pend.append(tpv)
                ppv[sl] = (pend[-1],)
            ev = []
            for qs in range(4):
                tcg = qt * 4 + qs
                tr = pb.emit("vector", lambda e, qs=qs: e.reciprocal(
                    bufs["rcol"][:, qs:qs + 1], ps_o[qs][:, 64:65]),
                    deps=[pend[-1]], strict=True)
                tn = pb.emit("vector", lambda e, qs=qs, tcg=tcg, h=h: e.tensor_scalar(
                    ctx[:, tcg * HDall + h * dh: tcg * HDall + (h + 1) * dh],
                    ps_o[qs][:, 0:64],
                    bufs["rcol"][:, qs:qs + 1], None, ALU.mult, ALU.bypass),
                    deps=[tr], strict=True)
                ev.append(tn)
            pso_use = tuple(ev)
            last_pv = pend[-1]
            x_toks.extend(ev)
        head_free[hf] = (last_pv,) + tuple(pso_use) + tuple(t for t in pexp if t)
    return x_toks


def build_attn_test(nq_tc=NTC, style="tanh"):
    nc = new_nc()
    es = ExitStack()
    T = nq_tc * 128
    KVT = 2 * T
    nkv = 2 * nq_tc
    H, dh = 16, 64
    tanh = (style == "tanh")
    lora = (style == "alibi")
    xin = nc.dram_tensor("xin", [nq_tc, 128, D], F32, kind="ExternalInput").ap()
    xprev = nc.dram_tensor("xprev", [nq_tc, 128, D], F32, kind="ExternalInput").ap()
    wq_d = nc.dram_tensor("wq", [H, DC, 128, dh], F32, kind="ExternalInput").ap()
    wk_d = nc.dram_tensor("wk", [H, DC, 128, dh], F32, kind="ExternalInput").ap()
    wv_d = nc.dram_tensor("wv", [H, DC, 128, dh], F32, kind="ExternalInput").ap()
    wo_d = nc.dram_tensor("wo", [DC, 128, D], F32, kind="ExternalInput").ap()
    ident = nc.dram_tensor("ident", [128, 128], BF16, kind="ExternalInput").ap()
    cmask_d = nc.dram_tensor("cmask", [128, 4 * 512], F32, kind="ExternalInput").ap()
    kmask_d = nc.dram_tensor("kmask", [128, nkv], F32, kind="ExternalInput").ap()
    kpos_d = nc.dram_tensor("kpos", [128, nkv], F32, kind="ExternalInput").ap()
    ones_d = nc.dram_tensor("onesc", [1, 128], BF16, kind="ExternalInput").ap()
    if tanh:
        bqk_d = nc.dram_tensor("bqk", [64, 2 * H], F32, kind="ExternalInput").ap()
        bvo_d = nc.dram_tensor("bvo", [1, 2 * D], BF16, kind="ExternalInput").ap()
    if lora:
        bT_d = {n2: nc.dram_tensor("bT" + n2, [DC, 128, 8], F32, kind="ExternalInput").ap()
                for n2 in ("q", "k", "v")}
        aT_d = {n2: nc.dram_tensor("aT" + n2, [8, D], F32, kind="ExternalInput").ap()
                for n2 in ("q", "k", "v")}
    xout = nc.dram_tensor("xout", [nq_tc, 128, D], F32, kind="ExternalOutput").ap()
    slopes = [2.0 ** (-0.5 - 0.5 * i) for i in range(H)] if lora else None

    with es:
        pb = PB(nc, es)
        stats = sb(pb, "stats", [128, 4 * nq_tc], F32)
        tmp_bf = sb(pb, "tmp_bf", [128, D], BF16)
        identb = sb(pb, "identb", [128, 128], BF16)
        onescol = sb(pb, "onescol", [1, 128], BF16)
        wstage = sb(pb, "wstage", [128, 1024], F32)
        rcol = sb(pb, "rcol", [128, 8], F32)
        kposb = sb(pb, "kposb", [128, nkv], F32)
        kmaskb = sb(pb, "kmaskb", [128, nkv], F32)
        cmaskb = sb(pb, "cmaskb", [128, 4 * 512], F32)
        ps_q = ps(pb, "ps_q", [128, 512])
        esKV = ExitStack()
        c_toks = [pb.dma("sync", identb[:, :], ident),
                  pb.dma("sync", onescol[0:1, :], ones_d),
                  pb.dma("sync", kposb[:, :], kpos_d),
                  pb.dma("sync", kmaskb[:, :], kmask_d),
                  pb.dma("sync", cmaskb[:, :], cmask_d)]
        bufs = dict(ps_q=ps_q,
                    onescol=onescol, wstage=wstage, rcol=rcol,
                    kpos=kposb, kmask=kmaskb, cmask=cmaskb)
        if tanh:
            bqk = sb(pb, "bqkb", [64, 2 * H], F32)
            bvo = sb(pb, "bvob", [1, 2 * D], BF16)
            c_toks += [pb.dma("sync", bqk[0:64, :], bqk_d),
                       pb.dma("sync", bvo[0:1, :], bvo_d)]
            bufs["bqk"] = bqk
            bufs["bqk_tok"] = c_toks[-2]
            bufs["bv_rows"] = bvo[0:1, 0:D]
        saved0 = pb.es
        pb.es = esKV
        xkv_fm = sb(pb, "xkv_fm", [128, DC * KVT], BF16)
        pb.es = saved0
        bufs["xkv_fm"] = xkv_fm
        esA = ExitStack()
        with esA:
            pbA_saved = pb.es
            pb.es = esA
            x = sb(pb, "x", [128, nq_tc * D], F32)
            dram_scr = sb(pb, "dram_scr", [128, 2 * D], F32)
            pt = (ps(pb, "ptA0", [128, 128], BF16), ps(pb, "ptA1", [128, 128], BF16))
            pb.es = pbA_saved
            x_toks = []
            for g in range(0, nq_tc, 4):
                gn = min(4, nq_tc - g)
                x_toks.append(pb.dma(
                    "gpsimd", x[:, g * D:(g + gn) * D].rearrange("p (t d) -> p t d", t=gn),
                    xin[g:g + gn].rearrange("t p d -> p t d")))
            ln1 = emit_ln_fm(pb, x, xkv_fm, stats, tmp_bf[:, 0:D], tmp_bf, identb, pt,
                             n_tc=nq_tc, tok_x=x_toks + [c_toks[0]],
                             dst_off=KVT // 2, dst_stride=KVT)
            ln2 = emit_ln_fm(pb, None, xkv_fm, stats, tmp_bf[:, 0:D], tmp_bf, identb, pt,
                             n_tc=nq_tc, tok_x=[ln1],
                             dst_off=0, dst_stride=KVT,
                             src_dram=[xprev[t] for t in range(nq_tc)],
                             dram_scr=dram_scr, dma_eng="gpsimd")
        entryB = pb.all_toks()
        esB = ExitStack()
        ctx_tm = sb(pb, "ctx_tm", [128, nq_tc * H * dh], BF16, side="right")
        with esB:
            saved = pb.es
            pb.es = esB
            ps_s0 = ps(pb, "ps_s0", [128, 512])
            ps_s1 = ps(pb, "ps_s1", [128, 512])
            ps_o = [ps(pb, "ps_o%d" % i, [128, 65]) for i in range(4)]
            bufs["ps_s0"] = ps_s0; bufs["ps_s1"] = ps_s1; bufs["ps_o"] = ps_o
            b2 = dict(
                qh0=sb(pb, "qh0", [64, T], BF16), qh1=sb(pb, "qh1", [64, T], BF16),
                kh0=sb(pb, "kh0", [64, KVT], BF16), kh1=sb(pb, "kh1", [64, KVT], BF16),
                vh0=sb(pb, "vh0", [128, nkv * 65], BF16), vh1=sb(pb, "vh1", [128, nkv * 65], BF16),
                wh0=sb(pb, "wh0", [128, 3 * 512], BF16), wh1=sb(pb, "wh1", [128, 3 * 512], BF16),
                pb0=sb(pb, "pb0", [128, 512], BF16), pb1=sb(pb, "pb1", [128, 512], BF16),
                bias0=sb(pb, "bias0", [128, nkv], F32), bias1=sb(pb, "bias1", [128, nkv], F32),
            )
            if lora:
                for n2 in ("q", "k", "v"):
                    b2["t1_" + n2] = sb(pb, "t1b" + n2, [8, KVT], BF16)
                    b2["bT_" + n2] = sb(pb, "bTb" + n2, [128, 64], BF16)
                    b2["aT_" + n2] = sb(pb, "aTb" + n2, [8, D], BF16)
            pb.es = saved
            bufs.update(b2)
            bufs["ctx_tm"] = ctx_tm
            bufs["xkv_tok"] = ln2
            bufs["const_tok"] = c_toks[-1]
            lw_toks = list(entryB)
            if lora:
                for n2 in ("q", "k", "v"):
                    tb1 = emit_weight_load_off(pb, [bT_d[n2][c] for c in range(DC)],
                                               wstage, bufs["bT_" + n2], 8, 0,
                                               deps=entryB, eng_dma="sync")
                    ta1 = pb.dma("sync", wstage[0:8, 0:D], aT_d[n2], deps=[tb1])
                    ta2 = pb.emit("scalar", lambda e, n2=n2: e.copy(
                        bufs["aT_" + n2][0:8, 0:D], wstage[0:8, 0:D]), deps=[ta1])
                    pb.tail[("stage", wstage.name)] = ta2
                    lw_toks.append(ta2)
            cfg = dict(nq_tc=nq_tc, heads=H, dh=dh, tanh=tanh, alibi=lora, lora=lora,
                       slopes=slopes,
                       wq=[[wq_d[h, c] for c in range(DC)] for h in range(H)],
                       wk=[[wk_d[h, c] for c in range(DC)] for h in range(H)],
                       wv=[[wv_d[h, c] for c in range(DC)] for h in range(H)])
            bufs["const_tok"] = c_toks[-1]
            bufs["lw_tok"] = tuple(lw_toks[len(entryB):]) if lora else ()
            attn_toks = emit_attn(pb, bufs, cfg)
        esKV.close()
        entryC = pb.all_toks()
        esC = ExitStack()
        with esC:
            saved = pb.es
            pb.es = esC
            pt = (ps(pb, "ptC0", [128, 128], BF16), ps(pb, "ptC1", [128, 128], BF16))
            ctx_fm = sb(pb, "ctx_fm", [128, DC * T], BF16)
            wo_bf = sb(pb, "wo_bf", [128, DC * 1024], BF16)
            projscr = sb(pb, "projscr", [128, 1024], F32)
            xr = sb(pb, "xr", [128, nq_tc * D], F32)
            pb.es = saved
            bufs["projscr"] = projscr
            xr_toks = []
            for g in range(0, nq_tc, 4):
                gn = min(4, nq_tc - g)
                xr_toks.append(pb.dma(
                    "gpsimd", xr[:, g * D:(g + gn) * D].rearrange("p (t d) -> p t d", t=gn),
                    xin[g:g + gn].rearrange("t p d -> p t d"), deps=entryC))
            wo_toks, wo_last = emit_weight_load(pb, [wo_d[c] for c in range(DC)],
                                                wstage, wo_bf, DC, 1024,
                                                deps=entryC, eng_dma="sync")
            fm_toks = emit_tm_to_fm(pb, ctx_tm, ctx_fm, identb, pt, n_tc=nq_tc,
                                    deps_per_tile=None, src_cols=H * dh,
                                    dst_cols=T, extra_deps=[attn_toks[-1]] + list(entryC))
            x_tok = {}
            for tc in range(nq_tc):
                for dc2 in range(2):
                    x_tok[(tc, dc2)] = xr_toks[tc // 4]
            pr_toks = emit_proj_residual(
                pb, bufs, ctx_fm, wo_bf, nq_tc, xr,
                act=(AF.Tanh if tanh else None),
                bias_row=(bvo[0:1, D:2 * D] if tanh else None),
                src_tok=[fm_toks[-1]], w_tok=[wo_last],
                x_tok=x_tok)
            st_toks = []
            for g in range(0, nq_tc, 4):
                gn = min(4, nq_tc - g)
                st_toks.append(pb.dma(
                    "gpsimd", xout[g:g + gn].rearrange("t p d -> p t d"),
                    xr[:, g * D:(g + gn) * D].rearrange("p (t d) -> p t d", t=gn),
                    deps=pr_toks))
            pb.emit("sync", lambda e: e.nop(), deps=st_toks)
        pb.build()
    return nc


def build_moe_l1_test(n_tc=NTC):
    nc = new_nc()
    es = ExitStack()
    T = n_tc * 128
    xin = nc.dram_tensor("xin", [n_tc, 128, D], F32, kind="ExternalInput").ap()
    gwT = nc.dram_tensor("gwT", [128, DC * 8], F32, kind="ExternalInput").ap()
    bias_d = nc.dram_tensor("ebias", [128, 8], F32, kind="ExternalInput").ap()
    gu = nc.dram_tensor("gu", [8, DC, 128, 2048], F32, kind="ExternalInput").ap()
    down = nc.dram_tensor("down", [8, DC, 128, 1024], F32, kind="ExternalInput").ap()
    sgu = nc.dram_tensor("sgu", [DC, 128, 2048], F32, kind="ExternalInput").ap()
    sdown = nc.dram_tensor("sdown", [DC, 128, 1024], F32, kind="ExternalInput").ap()
    ident = nc.dram_tensor("ident", [128, 128], BF16, kind="ExternalInput").ap()
    xout = nc.dram_tensor("xout", [n_tc, 128, D], F32, kind="ExternalOutput").ap()
    with es:
        pb = PB(nc, es)
        bufs = dict(
            x=sb(pb, "x", [128, n_tc * D], F32),
            xn_fm=sb(pb, "xn_fm", [128, DC * T], BF16),
            tmp_bf=sb(pb, "tmp_bf", [128, D], BF16),
            stats=sb(pb, "stats", [128, 4 * n_tc], F32),
            identb=sb(pb, "identb", [128, 128], BF16),
            gwT=sb(pb, "gwTb", [128, DC * 8], BF16),
            bias_bc=sb(pb, "bias_bc", [128, 8], F32),
            stage=sb(pb, "stage", [128, 1024], F32),
            stage2=sb(pb, "stage2", [128, 1024], F32),
            gu_bf0=sb(pb, "gu_bf0", [128, DC * 2048], BF16),
            gu_bf1=sb(pb, "gu_bf1", [128, DC * 2048], BF16),
            dn_bf0=sb(pb, "dn_bf0", [128, DC * 1024], BF16),
            hbuf0=sb(pb, "hbuf0", [128, 4096], BF16),
            hbuf1=sb(pb, "hbuf1", [128, 4096], BF16),
            sg=sb(pb, "sg", [128, 512], F32),
            routed_tm=sb(pb, "routed_tm", [128, n_tc * 8], F32),
            rscr=sb(pb, "rscr", [128, 64], F32),
            ps_r=ps(pb, "ps_r", [128, 8]),
            ps_g=ps(pb, "ps_g", [128, 512]),
            ps_u=ps(pb, "ps_u", [128, 512]),
            ps_d=ps(pb, "ps_d", [128, 1024]),
            pt=(ps(pb, "pt0", [128, 128], BF16), ps(pb, "pt1", [128, 128], BF16)),
        )
        tid = pb.dma("sync", bufs["identb"][:, :], ident)
        tbias = pb.dma("sync", bufs["bias_bc"][:, :], bias_d)
        bufs["bias_tok"] = tbias
        x_toks = []
        for g in range(0, n_tc, 4):
            gn = min(4, n_tc - g)
            x_toks.append(pb.dma(
                "sync",
                bufs["x"][:, g * D:(g + gn) * D].rearrange("p (t d) -> p t d", t=gn),
                xin[g:g + gn].rearrange("t p d -> p t d")))
        tgw = pb.dma("sync", bufs["rscr"][:, 0:64], gwT)
        tgwb = pb.emit("vector", lambda e: e.tensor_copy(bufs["gwT"][:, :], bufs["rscr"][:, 0:64]), deps=[tgw])
        fm_last = emit_ln_fm(pb, bufs["x"], bufs["xn_fm"], bufs["stats"],
                             bufs["tmp_bf"][:, 0:D],
                             bufs["tmp_bf"], bufs["identb"], bufs["pt"], n_tc=n_tc,
                             tok_x=x_toks + [tid, tgwb, tbias])
        wap = {
            "gu": [[gu[e_i, c // 2][:, (c % 2) * 1024:(c % 2) * 1024 + 1024] for c in range(DC * 2)] for e_i in range(8)],
            "down": [[down[e_i, c] for c in range(DC)] for e_i in range(8)],
            "sgu": [sgu[c // 2][:, (c % 2) * 1024:(c % 2) * 1024 + 1024] for c in range(DC * 2)],
            "sdown": [sdown[c] for c in range(DC)],
        }
        moe_toks = emit_moe_l0(pb, bufs, wap, n_tc=n_tc, fm_toks=[fm_last],
                               renorm=False, shared=True, l1=True)
        st_toks = []
        for g in range(0, n_tc, 4):
            gn = min(4, n_tc - g)
            st_toks.append(pb.dma(
                "gpsimd", xout[g:g + gn].rearrange("t p d -> p t d"),
                bufs["x"][:, g * D:(g + gn) * D].rearrange("p (t d) -> p t d", t=gn),
                deps=moe_toks))
        pb.emit("sync", lambda e: e.nop(), deps=st_toks)
        pb.build()
    return nc



def emit_ln_tm_f32(pb, x, out_tm, stats, sq_scr, n_tc=NTC, tok_x=()):
    n = n_tc
    t_red = []
    prev_sq = None
    for tc in range(n):
        xs = x[:, tc * D:(tc + 1) * D]
        dep1 = list(tok_x) + ([prev_sq] if prev_sq is not None else [])
        t1 = pb.emit("scalar", lambda e, xs=xs, tc=tc: e.activation(
            sq_scr[:, 0:D], xs, AF.Square, accum_out=stats[:, n + tc:n + tc + 1]),
            deps=dep1, strict=True)
        prev_sq = t1
        t2 = pb.emit("vector", lambda e, xs=xs, tc=tc: e.reduce_sum(
            stats[:, tc:tc + 1], xs, axis=AX.X), deps=tok_x)
        t_red += [t1, t2]
    tmu = pb.emit("vector", lambda e: e.tensor_scalar(
        stats[:, 2 * n:3 * n], stats[:, 0:n], 1.0 / D, None, ALU.mult, ALU.bypass),
        deps=t_red, strict=True)
    tm2 = pb.emit("vector", lambda e: e.tensor_tensor(
        stats[:, 0:n], stats[:, 2 * n:3 * n], stats[:, 2 * n:3 * n], ALU.mult),
        deps=[tmu], strict=True)
    tvar = pb.emit("vector", lambda e: e.scalar_tensor_tensor(
        stats[:, n:2 * n], stats[:, n:2 * n], 1.0 / D, stats[:, 0:n],
        ALU.mult, ALU.subtract), deps=[tm2], strict=True)
    tve = pb.emit("vector", lambda e: e.tensor_scalar(
        stats[:, n:2 * n], stats[:, n:2 * n], EPS_LN, None, ALU.add, ALU.bypass),
        deps=[tvar], strict=True)
    tln = pb.emit("scalar", lambda e: e.activation(
        stats[:, n:2 * n], stats[:, n:2 * n], AF.Ln, bias=0.0, scale=1.0),
        deps=[tve], strict=True)
    trstd = pb.emit("scalar", lambda e: e.activation(
        stats[:, 3 * n:4 * n], stats[:, n:2 * n], AF.Exp, bias=0.0, scale=-0.5),
        deps=[tln], strict=True)
    toks = []
    for tc in range(n):
        xs = x[:, tc * D:(tc + 1) * D]
        os = out_tm[:, tc * D:(tc + 1) * D]
        tt = pb.emit("vector", lambda e, xs=xs, os=os, tc=tc: e.tensor_scalar(
            os, xs, stats[:, 2 * n + tc:2 * n + tc + 1],
            stats[:, 3 * n + tc:3 * n + tc + 1], ALU.subtract, ALU.mult),
            deps=[trstd], strict=True)
        toks.append(tt)
    return toks


def emit_mingru(pb, bufs, cfg):
    n_tc = cfg["n_tc"]
    T = n_tc * 128
    L = cfg["layers"]
    x_mg = bufs["x_mg"]
    xn_fm = bufs["xn_fm"]
    W = bufs["w_bf"]
    psg, psu = bufs["ps_g"], bufs["ps_u"]
    ptF = bufs["ptF"]
    cbuf, ubuf, hbuf = bufs["cbuf"], bufs["ubuf"], bufs["hbuf"]
    zscr = bufs["zscr"]
    r1 = bufs["r1scr"]
    s1 = bufs["s1scr"]
    nb = bufs["nbcols"]
    bcols = bufs["bcols"]
    init = bufs["init"]
    hend = bufs["hend"]
    gb_tok = bufs["gb_tok"]
    x_toks = list(bufs["x_tok"])
    for l in range(L):
        g_bc, b_bc = bufs["g_bc"], bufs["b_bc"]
        bc_toks = []
        prev_bc = pb.tail.get(("mg_bc",), None)
        for which, dstt in ((0, g_bc), (1, b_bc)):
            for half in range(2):
                tm1 = pb.emit("tensor", lambda e, which=which, l=l, half=half, dstt=dstt: e.matmul(
                    bufs["ps_bc"][:, 0:512], bufs["onescol"][0:1, 0:128],
                    bufs["gbrows"][0:1, (which * L + l) * D + half * 512:
                                   (which * L + l) * D + half * 512 + 512],
                    start=True, stop=True),
                    deps=[gb_tok] + ([prev_bc] if prev_bc else []))
                prev_bc = pb.emit("scalar", lambda e, dstt=dstt, half=half: e.copy(
                    dstt[:, half * 512:half * 512 + 512], bufs["ps_bc"][:, 0:512]),
                    deps=[tm1])
                bc_toks.append(prev_bc)
        pb.tail[("mg_bc",)] = prev_bc
        lnt = emit_ln_fm(pb, x_mg, xn_fm, bufs["stats"], bufs["tmp_bf"][:, 0:D],
                         bufs["tmp_bf"], bufs["identb"], bufs["pt"], n_tc=n_tc,
                         tok_x=x_toks + bc_toks,
                         affine_wb=(g_bc, b_bc, bufs["lnscr"]))
        wl_tok = emit_weight_load_off(pb, cfg["w"][l], bufs["stage"], W, 1024, 0,
                                      deps=pb.tail.get(("mg_wfree",), ()) or (),
                                      eng_dma="sync")
        x_toks2 = []
        scan_toks = []
        for cc in range(DC):
            cu_toks = []
            for tt in range(n_tc // 4):
                mmg = mmh = None
                for dc in range(DC):
                    dep = [wl_tok, lnt]
                    if dc == 0 and ("mg_pg",) in pb.tail:
                        dep = dep + [pb.tail[("mg_pg",)]]
                    mmg = pb.emit("tensor", lambda e, l=l, cc=cc, tt=tt, dc=dc: e.matmul(
                        psg[:, 0:512],
                        W[:, dc * 2048 + cc * 128: dc * 2048 + cc * 128 + 128],
                        xn_fm[:, dc * T + tt * 512: dc * T + tt * 512 + 512],
                        start=(dc == 0), stop=(dc == DC - 1)), deps=dep)
                for dc in range(DC):
                    dep = [wl_tok, lnt]
                    if dc == 0 and ("mg_ph",) in pb.tail:
                        dep = dep + [pb.tail[("mg_ph",)]]
                    mmh = pb.emit("tensor", lambda e, l=l, cc=cc, tt=tt, dc=dc: e.matmul(
                        psu[:, 0:512],
                        W[:, dc * 2048 + 1024 + cc * 128: dc * 2048 + 1024 + cc * 128 + 128],
                        xn_fm[:, dc * T + tt * 512: dc * T + tt * 512 + 512],
                        start=(dc == 0), stop=(dc == DC - 1)), deps=dep)
                bcol = bcols[:, l * 16 + cc: l * 16 + cc + 1]
                nbcol = nb[:, l * 16 + cc: l * 16 + cc + 1]
                bcol2 = bcols[:, l * 16 + 8 + cc: l * 16 + 8 + cc + 1]
                tc1 = pb.emit("scalar", lambda e, cc=cc, tt=tt, nbcol=nbcol: e.activation(
                    cbuf[:, tt * 512:tt * 512 + 512], psg[:, 0:512], AF.Sigmoid,
                    bias=nbcol, scale=-1.0),
                    deps=[mmg, bufs["nb_tok"]] + list(pb.tail.get(("mg_cu",), ()))
                         + list(pb.tail.get(("mg_scan",), ())), strict=True)
                tz = pb.emit("scalar", lambda e, cc=cc, tt=tt, bcol=bcol: e.activation(
                    zscr[:, 0:512], psg[:, 0:512], AF.Sigmoid, bias=bcol, scale=1.0),
                    deps=[tc1, bufs["bc_tok"]], strict=True)
                pb.tail[("mg_pg",)] = tz
                tr1 = pb.emit("scalar", lambda e, bcol2=bcol2: e.activation(
                    r1[:, 0:512], psu[:, 0:512], AF.Relu, bias=bcol2, scale=1.0),
                    deps=[mmh, bufs["bc_tok"], tz], strict=True)
                ts1 = pb.emit("scalar", lambda e, bcol2=bcol2: e.activation(
                    s1[:, 0:512], psu[:, 0:512], AF.Sigmoid, bias=bcol2, scale=1.0),
                    deps=[tr1], strict=True)
                pb.tail[("mg_ph",)] = ts1
                tm1_ = pb.emit("vector", lambda e: e.tensor_scalar_min(
                    s1[:, 0:512], s1[:, 0:512], 0.5), deps=[ts1], strict=True)
                tg = pb.emit("vector", lambda e: e.tensor_tensor(
                    r1[:, 0:512], r1[:, 0:512], s1[:, 0:512], ALU.add),
                    deps=[tm1_, tr1], strict=True)
                tu = pb.emit("vector", lambda e, tt=tt: e.tensor_tensor(
                    ubuf[:, tt * 512:tt * 512 + 512], zscr[:, 0:512], r1[:, 0:512],
                    ALU.mult), deps=[tg, tz] + list(pb.tail.get(("mg_scan",), ())),
                    strict=True)
                pb.tail[("mg_cu",)] = (tu,)
                cu_toks.append(tu)
                cu_toks.append(tc1)
            icol = init[:, l * 8 + cc: l * 8 + cc + 1]
            tscan = pb.emit("vector", lambda e, cc=cc, icol=icol: e.tensor_tensor_scan(
                hbuf[:, 0:T], cbuf[:, 0:T], ubuf[:, 0:T], icol, ALU.mult, ALU.add),
                deps=cu_toks + [bufs["init_tok"]] + list(pb.tail.get(("mg_h",), ())),
                strict=True)
            scan_toks.append(tscan)
            pb.tail[("mg_scan",)] = (tscan,)
            th = pb.emit("vector", lambda e, l=l, cc=cc: e.tensor_copy(
                hend[:, l * 8 + cc: l * 8 + cc + 1], hbuf[:, T - 1:T]),
                deps=[tscan], strict=True)
            add_toks = []
            for tc2 in range(n_tc):
                ph = tc2 % 2
                tpt = pb.emit("tensor", lambda e, tc2=tc2, ph=ph: e.transpose(
                    ptF[ph][:, 0:128], hbuf[:, tc2 * 128:tc2 * 128 + 128],
                    bufs["identf"][:, 0:128]),
                    deps=[tscan] + list(pb.tail.get(("mg_pt%d" % ph,), ())))
                ta = pb.emit("vector", lambda e, tc2=tc2, cc=cc, ph=ph: e.tensor_tensor(
                    x_mg[:, tc2 * D + cc * 128: tc2 * D + cc * 128 + 128],
                    ptF[ph][:, 0:128],
                    x_mg[:, tc2 * D + cc * 128: tc2 * D + cc * 128 + 128], ALU.add),
                    deps=[tpt], strict=True)
                pb.tail[("mg_pt%d" % ph,)] = (ta,)
                add_toks.append(ta)
            pb.tail[("mg_h",)] = (add_toks[-1],) + (th,)
            x_toks2.extend(add_toks)
        pb.tail[("mg_wfree",)] = (x_toks2[-1],)
        x_toks = x_toks2
    return x_toks


def build_mingru_test(n_tc=NTC):
    nc = new_nc()
    es = ExitStack()
    T = n_tc * 128
    L = 3
    xin = nc.dram_tensor("xin", [n_tc, 128, D], F32, kind="ExternalInput").ap()
    w_d = nc.dram_tensor("w", [L, DC, 128, 2048], F32, kind="ExternalInput").ap()
    gb_d = nc.dram_tensor("gbrows", [1, 2 * L * D], BF16, kind="ExternalInput").ap()
    bc_d = nc.dram_tensor("bcols", [128, L * 16], F32, kind="ExternalInput").ap()
    init_d = nc.dram_tensor("initc", [128, L * 8], F32, kind="ExternalInput").ap()
    ident = nc.dram_tensor("ident", [128, 128], BF16, kind="ExternalInput").ap()
    identf_d = nc.dram_tensor("identf_d", [128, 128], F32, kind="ExternalInput").ap()
    ones_d = nc.dram_tensor("onesc", [1, 128], BF16, kind="ExternalInput").ap()
    xout = nc.dram_tensor("xout", [n_tc, 128, D], F32, kind="ExternalOutput").ap()
    hend_d = nc.dram_tensor("hend", [128, L * 8], F32, kind="ExternalOutput").ap()
    with es:
        pb = PB(nc, es)
        bufs = dict(
            x_mg=sb(pb, "x_mg", [128, n_tc * D], F32),
            tmp_bf=sb(pb, "tmp_bf", [128, D], BF16),
            stats=sb(pb, "stats", [128, 4 * n_tc], F32),
            identb=sb(pb, "identb", [128, 128], BF16),
            identf=sb(pb, "identf", [128, 128], F32),
            onescol=sb(pb, "onescol", [1, 128], BF16),
            gbrows=sb(pb, "gbrows_b", [1, 2 * L * D], BF16),
            bcols=sb(pb, "bcols_b", [128, L * 16], F32),
            nbcols=sb(pb, "nbcols", [128, L * 16], F32),
            init=sb(pb, "init_b", [128, L * 8], F32),
            hend=sb(pb, "hend_b", [128, L * 8], F32),
            g_bc=sb(pb, "g_bc", [128, D], F32),
            b_bc=sb(pb, "b_bc", [128, D], F32),
            lnscr=sb(pb, "lnscr", [128, D], F32),
            ps_g=ps(pb, "ps_g", [128, 512]),
            ps_u=ps(pb, "ps_u", [128, 512]),
            ps_bc=ps(pb, "ps_bc", [128, 512]),
            pt=(ps(pb, "pt0", [128, 128], BF16), ps(pb, "pt1", [128, 128], BF16)),
            ptF=(ps(pb, "ptF0", [128, 128], F32), ps(pb, "ptF1", [128, 128], F32)),
        )
        c1 = pb.dma("sync", bufs["identb"][:, :], ident)
        c2 = pb.dma("sync", bufs["identf"][:, :], identf_d)
        c3 = pb.dma("sync", bufs["onescol"][0:1, :], ones_d)
        c4 = pb.dma("sync", bufs["gbrows"][0:1, :], gb_d)
        c5 = pb.dma("sync", bufs["bcols"][:, :], bc_d)
        c6 = pb.dma("sync", bufs["init"][:, :], init_d)
        tnb = pb.emit("vector", lambda e: e.tensor_scalar(
            bufs["nbcols"][:, :], bufs["bcols"][:, :], -1.0, None, ALU.mult, ALU.bypass),
            deps=[c5])
        bufs["gb_tok"] = c4
        bufs["bc_tok"] = c5
        bufs["nb_tok"] = tnb
        bufs["init_tok"] = c6
        esA = ExitStack()
        with esA:
            saved = pb.es; pb.es = esA
            x = sb(pb, "x", [128, n_tc * D], F32)
            pb.es = saved
            x_toks = []
            for g in range(0, n_tc, 4):
                gn = min(4, n_tc - g)
                x_toks.append(pb.dma(
                    "gpsimd", x[:, g * D:(g + gn) * D].rearrange("p (t d) -> p t d", t=gn),
                    xin[g:g + gn].rearrange("t p d -> p t d")))
            ln_toks = emit_ln_tm_f32(pb, x, bufs["x_mg"], bufs["stats"],
                                     bufs["lnscr"], n_tc=n_tc, tok_x=x_toks + [c6])
        cfg = dict(n_tc=n_tc, layers=L,
                   w=[[w_d[l, c // 2][:, (c % 2) * 1024:(c % 2) * 1024 + 1024]
                       for c in range(DC * 2)] for l in range(L)])
        bufs["x_tok"] = ln_toks
        esB = ExitStack()
        with esB:
            saved = pb.es; pb.es = esB
            bufs["xn_fm"] = sb(pb, "xn_fm", [128, DC * T], BF16)
            bufs["w_bf"] = sb(pb, "w_bf", [128, DC * 2048], BF16)
            bufs["stage"] = sb(pb, "stage", [128, 2048], F32)
            bufs["cbuf"] = sb(pb, "cbuf", [128, T], F32)
            bufs["ubuf"] = sb(pb, "ubuf", [128, T], F32)
            bufs["hbuf"] = sb(pb, "hbuf", [128, T], F32)
            bufs["zscr"] = sb(pb, "zscr", [128, 512], F32)
            bufs["r1scr"] = sb(pb, "r1scr", [128, 512], F32)
            bufs["s1scr"] = sb(pb, "s1scr", [128, 512], F32)
            pb.es = saved
            mg_toks = emit_mingru(pb, bufs, cfg)
        entryC = pb.all_toks()
        esC = ExitStack()
        with esC:
            saved = pb.es; pb.es = esC
            xr = sb(pb, "xr", [128, n_tc * D], F32)
            pb.es = saved
            xr_toks = []
            for g in range(0, n_tc, 4):
                gn = min(4, n_tc - g)
                xr_toks.append(pb.dma(
                    "gpsimd", xr[:, g * D:(g + gn) * D].rearrange("p (t d) -> p t d", t=gn),
                    xin[g:g + gn].rearrange("t p d -> p t d"), deps=entryC))
            add_toks = []
            for tc in range(n_tc):
                ta = pb.emit("vector", lambda e, tc=tc: e.tensor_tensor(
                    xr[:, tc * D:(tc + 1) * D], bufs["x_mg"][:, tc * D:(tc + 1) * D],
                    xr[:, tc * D:(tc + 1) * D], ALU.add),
                    deps=[xr_toks[tc // 4]] + mg_toks[-1:], strict=True)
                add_toks.append(ta)
            st = []
            for g in range(0, n_tc, 4):
                gn = min(4, n_tc - g)
                st.append(pb.dma(
                    "gpsimd", xout[g:g + gn].rearrange("t p d -> p t d"),
                    xr[:, g * D:(g + gn) * D].rearrange("p (t d) -> p t d", t=gn),
                    deps=add_toks))
            st.append(pb.dma("sync", hend_d, bufs["hend"][:, :], deps=mg_toks))
            pb.emit("sync", lambda e: e.nop(), deps=st)
        pb.build()
    return nc



def emit_ln_stream(pb, bufs, src_aps, dst_aps, n_tc, tok0=()):
    scr = [bufs["lnt0"], bufs["lnt1"]]
    cs = bufs["cstats"]
    free = {0: (), 1: ()}
    toks = []
    for tc in range(n_tc):
        half = tc % 2
        s = scr[half]
        tl = pb.dma("gpsimd", s[:, 0:D], src_aps[tc], deps=list(tok0) + list(free[half]))
        t1 = pb.emit("scalar", lambda e, s=s: e.activation(
            bufs["sqscr"][:, 0:D], s[:, 0:D], AF.Square, accum_out=cs[:, 1:2]),
            deps=[tl] + list(pb.tail.get(("lns_sq",), ())), strict=True)
        pb.tail[("lns_sq",)] = (t1,)
        t2 = pb.emit("vector", lambda e, s=s: e.reduce_sum(cs[:, 0:1], s[:, 0:D], axis=AX.X),
                     deps=[tl] + list(pb.tail.get(("lns_cs",), ())), strict=True)
        t3 = pb.emit("vector", lambda e: e.tensor_scalar(cs[:, 2:3], cs[:, 0:1], 1.0 / D, None, ALU.mult, ALU.bypass), deps=[t2], strict=True)
        t4 = pb.emit("vector", lambda e: e.tensor_tensor(cs[:, 3:4], cs[:, 2:3], cs[:, 2:3], ALU.mult), deps=[t3], strict=True)
        t5 = pb.emit("vector", lambda e: e.scalar_tensor_tensor(cs[:, 4:5], cs[:, 1:2], 1.0 / D, cs[:, 3:4], ALU.mult, ALU.subtract), deps=[t4, t1], strict=True)
        t6 = pb.emit("vector", lambda e: e.tensor_scalar(cs[:, 4:5], cs[:, 4:5], EPS_LN, None, ALU.add, ALU.bypass), deps=[t5], strict=True)
        t7 = pb.emit("scalar", lambda e: e.activation(cs[:, 5:6], cs[:, 4:5], AF.Ln, bias=0.0, scale=1.0), deps=[t6], strict=True)
        t8 = pb.emit("scalar", lambda e: e.activation(cs[:, 6:7], cs[:, 5:6], AF.Exp, bias=0.0, scale=-0.5), deps=[t7], strict=True)
        t9 = pb.emit("vector", lambda e, s=s: e.tensor_scalar(
            s[:, 0:D], s[:, 0:D], cs[:, 2:3], cs[:, 6:7], ALU.subtract, ALU.mult),
            deps=[t8], strict=True)
        pb.tail[("lns_cs",)] = (t9,)
        ts = pb.dma("gpsimd", dst_aps[tc], s[:, 0:D], deps=[t9])
        free[half] = (ts,)
        toks.append(ts)
    return toks


def emit_retention_iter_p2(pb, bufs, cfg, ln1):
    n_tc = cfg["n_tc"]
    T = n_tc * 128
    H = 8
    HD = 128
    zi, zo, yd, x1 = cfg["zi"], cfg["zo"], cfg["yd"], cfg["x1"]
    xkv = bufs["xkv_fm"]
    it = cfg["iter"]
    entry0 = tuple(cfg.get("entry", ()))
    wb = bufs["whb"]
    abuf = bufs["abuf"]
    obuf = bufs["obuf"]
    obuf2 = bufs["obuf2"]
    ptB = bufs["ptB"]
    Q, K = bufs["q_fm"], bufs["k_fm"]
    QR, KR = bufs["qrot"], bufs["krot"]
    KH = bufs["khat"]
    V = bufs["v_tm"]
    TS1 = bufs["tscr1"]
    St = bufs["state"]
    Sbf = bufs["state_bf"]
    psA = bufs["ps_s0"]
    psB = bufs["ps_s1"]
    psO1 = bufs["ps_o1"]
    psO2 = bufs["ps_o2"]
    psSU = bufs["ps_su"]
    head_tok = entry0
    yd_toks = []
    for h in range(H):
        wl = {}
        for wi, name in enumerate(("wq", "wk", "wv")):
            wl[name] = emit_weight_load_off(pb, cfg[name][h], bufs["wstage"],
                                            bufs["whb"], 128, wi * 1024,
                                            deps=head_tok, eng_dma="sync")
        pq = {}
        for wi, (name, dst) in enumerate((("wq", Q), ("wk", K))):
            last = None
            for tt in range(T // 512):
                mm = None
                for dc2 in range(DC):
                    dep = [wl[name], ln1] + list(head_tok) + ([pq["last"]] if (dc2 == 0 and pq.get("last")) else [])
                    mm = pb.emit("tensor", lambda e, wi=wi, tt=tt, dc2=dc2, wb=wb: e.matmul(
                        psA[:, 0:512],
                        wb[:, wi * 1024 + dc2 * 128: wi * 1024 + dc2 * 128 + 128],
                        xkv[:, dc2 * T + tt * 512: dc2 * T + tt * 512 + 512],
                        start=(dc2 == 0), stop=(dc2 == DC - 1)), deps=dep)
                last = pb.emit("scalar", lambda e, dst=dst, tt=tt: e.copy(
                    dst[:, tt * 512:tt * 512 + 512], psA[:, 0:512]), deps=[mm], strict=True)
                pq[name] = last
                pq["last"] = last
        rot_toks = []
        rp = list(pb.tail.get(("ret_rot",), ()))
        tab = bufs["tab"]
        TSF = bufs["tscr1"]
        for pi, (src, dst) in enumerate(((Q, QR), (K, KR))):
            tld = [pb.dma("sync", tab[:, 0:T], cfg["tabs_d"][2 * pi],
                          deps=rp + list(head_tok) + [pq["last"]]),
                   pb.dma("sync", tab[:, T:2 * T], cfg["tabs_d"][2 * pi + 1],
                          deps=rp + list(head_tok) + [pq["last"]])]
            tsw1 = pb.dma("gpsimd", TSF[0:64, 0:T], src[64:128, 0:T],
                          deps=[pq["last"]] + rp + list(head_tok))
            tsw2 = pb.dma("gpsimd", TSF[64:128, 0:T], src[0:64, 0:T],
                          deps=[pq["last"]] + rp + list(head_tok))
            base = [pq["wq"], pq["wk"]] + tld + [tsw1, tsw2]
            tq1 = pb.emit("vector", lambda e, src=src, dst=dst, tab=tab: e.tensor_tensor(
                dst[:, 0:T], src[:, 0:T], tab[:, 0:T], ALU.mult),
                deps=base, strict=True)
            tq2 = pb.emit("vector", lambda e, tab=tab, TSF=TSF: e.tensor_tensor(
                TSF[:, 0:T], TSF[:, 0:T], tab[:, T:2 * T], ALU.mult),
                deps=base, strict=True)
            tq3 = pb.emit("vector", lambda e, dst=dst, TSF=TSF: e.tensor_tensor(
                dst[0:64, 0:T], dst[0:64, 0:T], TSF[0:64, 0:T], ALU.subtract),
                deps=[tq1, tq2], strict=True)
            tq4 = pb.emit("vector", lambda e, dst=dst, TSF=TSF: e.tensor_tensor(
                dst[64:128, 0:T], dst[64:128, 0:T], TSF[64:128, 0:T], ALU.add),
                deps=[tq3], strict=True)
            rot_toks.append(tq4)
            rp = [tq4]
        pb.tail[("ret_rot",)] = (rot_toks[-1],)
        pv = None
        kh_toks = []
        for kc in range(n_tc):
            mm = None
            for dc2 in range(DC):
                dep = [wl["wv"], ln1] + list(head_tok) + ([pv] if (dc2 == 0 and pv is not None) else [])
                mm = pb.emit("tensor", lambda e, kc=kc, dc2=dc2, wb=wb: e.matmul(
                    psB[:, 0:128],
                    xkv[:, dc2 * T + kc * 128: dc2 * T + kc * 128 + 128],
                    wb[:, 2 * 1024 + dc2 * 128: 2 * 1024 + dc2 * 128 + 128],
                    start=(dc2 == 0), stop=(dc2 == DC - 1)), deps=dep)
            pv = pb.emit("scalar", lambda e, kc=kc: e.copy(
                V[:, kc * 128:kc * 128 + 128], psB[:, 0:128]), deps=[mm], strict=True)
            tkt = pb.emit("tensor", lambda e, kc=kc, ptB=ptB: e.transpose(
                ptB[kc % 2][:, 0:128], KR[:, kc * 128:kc * 128 + 128],
                bufs["identb"][:, 0:128]),
                deps=rot_toks + list(pb.tail.get(("ret_kh%d" % (kc % 2),), ())))
            tke = pb.emit("vector", lambda e, kc=kc, h=h, ptB=ptB: e.tensor_scalar(
                KH[:, kc * 128:kc * 128 + 128], ptB[kc % 2][:, 0:128],
                bufs["ghcol"][:, h:h + 1], None, ALU.mult, ALU.bypass),
                deps=[tkt, cfg["dk_tok"]], strict=True)
            pb.tail[("ret_kh%d" % (kc % 2),)] = (tke,)
            kh_toks.append(tke)
        tst = pb.emit("vector", lambda e: e.memset(St[:, :], 0.0), deps=list(head_tok))
        st_tok = tst
        sbf_tok = pb.emit("vector", lambda e: e.tensor_copy(Sbf[:, :], St[:, :]),
                          deps=[tst], strict=True)
        prev_A = ()
        prev_o = ()
        for kc in range(n_tc):
            tA = pb.emit("tensor", lambda e, kc=kc: e.matmul(
                psA[:, 0:128], KR[:, kc * 128:kc * 128 + 128],
                QR[:, kc * 128:kc * 128 + 128], start=True, stop=True),
                deps=rot_toks + list(prev_A) + [pq["last"]])
            tAm = pb.emit("vector", lambda e, kc=kc, h=h: e.tensor_tensor(
                abuf[:, 0:128], psA[:, 0:128],
                bufs["dmask"][:, h * 128:(h + 1) * 128], ALU.mult),
                deps=[tA, cfg["dk_tok"]] + list(pb.tail.get(("ret_ab",), ())), strict=True)
            prev_A = (tAm,)
            t1_ = pb.emit("tensor", lambda e, kc=kc: e.matmul(
                psO1[:, 0:128], abuf[:, 0:128], V[:, kc * 128:kc * 128 + 128],
                start=True, stop=True), deps=[tAm, pv] + list(prev_o))
            pb.tail[("ret_ab",)] = (t1_,)
            t2_ = pb.emit("tensor", lambda e, kc=kc: e.matmul(
                psO2[:, 0:128], QR[:, kc * 128:kc * 128 + 128], Sbf[:, 0:128],
                start=True, stop=True), deps=[sbf_tok] + list(prev_o))
            t3_ = pb.emit("tensor", lambda e, kc=kc: e.matmul(
                psSU[:, 0:128], KH[:, kc * 128:kc * 128 + 128],
                V[:, kc * 128:kc * 128 + 128], start=True, stop=True),
                deps=[kh_toks[kc], pv] + list(pb.tail.get(("ret_su",), ())))
            to1 = pb.emit("vector", lambda e, h=h: e.tensor_scalar(
                obuf[:, 0:128], psO2[:, 0:128], bufs["gcol"][:, h:h + 1],
                None, ALU.mult, ALU.bypass),
                deps=[t2_, cfg["dk_tok"]] + list(pb.tail.get(("ret_ob",), ())), strict=True)
            to2 = pb.emit("vector", lambda e: e.tensor_tensor(
                obuf[:, 0:128], obuf[:, 0:128], psO1[:, 0:128], ALU.add),
                deps=[to1, t1_], strict=True)
            prev_o = (to2,)
            ts1_ = pb.emit("vector", lambda e, h=h: e.scalar_tensor_tensor(
                St[:, :], St[:, :], cfg["gC"][h], psSU[:, 0:128], ALU.mult, ALU.add),
                deps=[t3_, st_tok, sbf_tok], strict=True)
            pb.tail[("ret_su",)] = (ts1_,)
            st_tok = ts1_
            sbf_tok = pb.emit("vector", lambda e: e.tensor_copy(Sbf[:, :], St[:, :]),
                              deps=[ts1_, t2_], strict=True)
            g1 = pb.emit("vector", lambda e: e.reduce_sum(
                bufs["cstats"][:, 0:1], obuf[:, 0:128], axis=AX.X),
                deps=[to2] + list(pb.tail.get(("ret_gn",), ())), strict=True)
            g2 = pb.emit("scalar", lambda e: e.activation(
                bufs["sqscr"][:, 0:128], obuf[:, 0:128], AF.Square,
                accum_out=bufs["cstats"][:, 1:2]),
                deps=[to2] + list(pb.tail.get(("ret_gn_sq",), ()))
                     + list(pb.tail.get(("ret_gn",), ())), strict=True)
            pb.tail[("ret_gn_sq",)] = (g2,)
            g3 = pb.emit("vector", lambda e: e.tensor_scalar(
                bufs["cstats"][:, 2:3], bufs["cstats"][:, 0:1], 1.0 / HD, None,
                ALU.mult, ALU.bypass), deps=[g1], strict=True)
            g4 = pb.emit("vector", lambda e: e.tensor_tensor(
                bufs["cstats"][:, 3:4], bufs["cstats"][:, 2:3], bufs["cstats"][:, 2:3],
                ALU.mult), deps=[g3], strict=True)
            g5 = pb.emit("vector", lambda e: e.scalar_tensor_tensor(
                bufs["cstats"][:, 4:5], bufs["cstats"][:, 1:2], 1.0 / HD,
                bufs["cstats"][:, 3:4], ALU.mult, ALU.subtract), deps=[g4, g2], strict=True)
            g6 = pb.emit("vector", lambda e: e.tensor_scalar(
                bufs["cstats"][:, 4:5], bufs["cstats"][:, 4:5], EPS_LN, None,
                ALU.add, ALU.bypass), deps=[g5], strict=True)
            g7 = pb.emit("scalar", lambda e: e.activation(
                bufs["cstats"][:, 5:6], bufs["cstats"][:, 4:5], AF.Ln, bias=0.0, scale=1.0),
                deps=[g6], strict=True)
            g8 = pb.emit("scalar", lambda e: e.activation(
                bufs["cstats"][:, 6:7], bufs["cstats"][:, 5:6], AF.Exp, bias=0.0, scale=-0.5),
                deps=[g7], strict=True)
            g9 = pb.emit("vector", lambda e: e.tensor_scalar(
                obuf[:, 0:128], obuf[:, 0:128], bufs["cstats"][:, 2:3],
                bufs["cstats"][:, 6:7], ALU.subtract, ALU.mult), deps=[g8], strict=True)
            pb.tail[("ret_gn",)] = (g9,)
            g10 = pb.emit("vector", lambda e, h=h: e.tensor_tensor(
                obuf[:, 0:128], obuf[:, 0:128],
                bufs["gnw_bc"][:, h * 128:(h + 1) * 128], ALU.mult),
                deps=[g9, cfg["gn_tok"]], strict=True)
            g11 = pb.emit("vector", lambda e, h=h: e.tensor_tensor(
                obuf2[:, 0:128], obuf[:, 0:128],
                bufs["gnb_bc"][:, h * 128:(h + 1) * 128], ALU.add),
                deps=[g10] + list(pb.tail.get(("ret_st",), ())), strict=True)
            pb.tail[("ret_ob",)] = (g11,)
            tstore = pb.dma("gpsimd", yd[kc][:, h * 128:h * 128 + 128],
                            obuf2[:, 0:128], deps=[g11])
            pb.tail[("ret_st",)] = (tstore,)
            yd_toks.append(tstore)
        head_tok = (yd_toks[-1], sbf_tok) + tuple(rot_toks)
    return ln1, yd_toks


def emit_gb_bcast(pb, bufs, g_off, b_off, dst_g=None, dst_b=None, cols=D):
    dst_g = dst_g if dst_g is not None else bufs["g_bc"]
    dst_b = dst_b if dst_b is not None else bufs["b_bc"]
    psbc = bufs["ps_bc"]
    onesc = bufs["onescol"]
    gbr = bufs["gbrows"]
    prev = list(pb.tail.get(("gb_bc",), ())) + [pb.barrier_tok("vector")]
    toks = []
    for off, dstt in ((g_off, dst_g), (b_off, dst_b)):
        for half in range(cols // 512):
            tm1 = pb.emit("tensor", lambda e, off=off, half=half, psbc=psbc, gbr=gbr, onesc=onesc: e.matmul(
                psbc[:, 0:512], onesc[0:1, 0:128],
                gbr[0:1, off + half * 512: off + half * 512 + 512],
                start=True, stop=True),
                deps=[bufs["gb_tok"]] + prev)
            tcp = pb.emit("scalar", lambda e, dstt=dstt, half=half, psbc=psbc: e.copy(
                dstt[:, half * 512:half * 512 + 512], psbc[:, 0:512]),
                deps=[tm1])
            prev = [tcp]
            toks.append(tcp)
    pb.tail[("gb_bc",)] = (toks[-1],)
    return toks


def emit_retention_p3(pb, bufs, cfg, yd_toks, ln1_tok):
    n_tc = cfg["n_tc"]
    T = n_tc * 128
    xkv = bufs["xkv_fm"]
    wg, wo = bufs["wg_bf"], bufs["wo_bf"]
    psA = bufs["ps_s0"]
    psB = bufs["ps_s1"]
    ytile = bufs["ytile"]
    gtile = bufs["gtile"]
    yfm = bufs["yfm"]
    ztile = bufs["zt"]
    ptB = bufs["ptB"]
    x1_toks = []
    ydep = [yd_toks[-1]]
    for tc in range(n_tc):
        ty = pb.dma("gpsimd", ytile[:, 0:D], cfg["yd"][tc],
                    deps=ydep + list(pb.tail.get(("p3_y",), ())))
        tz = pb.dma("sync", ztile[:, 0:D], cfg["zi"][tc],
                    deps=list(pb.tail.get(("p3_z",), ())))
        tr_toks = []
        for dc2 in range(2):
            mm = None
            for dc in range(DC):
                dep = [cfg["wg_tok"], ln1_tok] + (list(pb.tail.get(("p3_g",), ())) if dc == 0 else [])
                mm = pb.emit("tensor", lambda e, tc=tc, dc=dc, dc2=dc2: e.matmul(
                    psA[:, 0:512], xkv[:, dc * T + tc * 128: dc * T + tc * 128 + 128],
                    wg[:, dc * 1024 + dc2 * 512: dc * 1024 + dc2 * 512 + 512],
                    start=(dc == 0), stop=(dc == DC - 1)), deps=dep)
            tg = pb.emit("scalar", lambda e: e.activation(
                gtile[:, 0:512], psA[:, 0:512], AF.Silu),
                deps=[mm] + list(pb.tail.get(("p3_gt",), ())), strict=True)
            tmul = pb.emit("vector", lambda e, dc2=dc2: e.tensor_tensor(
                ytile[:, dc2 * 512:dc2 * 512 + 512], gtile[:, 0:512],
                ytile[:, dc2 * 512:dc2 * 512 + 512], ALU.mult),
                deps=[tg, ty], strict=True)
            pb.tail[("p3_g",)] = (tmul,)
            pb.tail[("p3_gt",)] = (tmul,)
            tr_toks.append(tmul)
        tps = []
        for dc in range(DC):
            ph = dc % 2
            tt1 = pb.emit("tensor", lambda e, dc=dc, ph=ph, ptB=ptB: e.transpose(
                ptB[ph][:, 0:128], ytile[:, dc * 128:dc * 128 + 128],
                bufs["identb"][:, 0:128]),
                deps=tr_toks + list(pb.tail.get(("p3_pt%d" % ph,), ())))
            tt2 = pb.emit("scalar", lambda e, dc=dc, ph=ph, ptB=ptB: e.copy(
                yfm[:, dc * 128:dc * 128 + 128], ptB[ph][:, 0:128]), deps=[tt1],
                strict=True)
            pb.tail[("p3_pt%d" % ph,)] = (tt2,)
            tps.append(tt2)
        pb.tail[("p3_y",)] = (tps[-1],)
        st_toks = []
        for dc2 in range(2):
            mm = None
            for cf in range(DC):
                dep = [cfg["wo_tok"], tps[-1]] + (list(pb.tail.get(("p3_wo%d" % dc2,), ())) if cf == 0 else [])
                mm = pb.emit("tensor", lambda e, cf=cf, dc2=dc2: e.matmul(
                    psB[:, dc2 * 512:dc2 * 512 + 512], yfm[:, cf * 128:cf * 128 + 128],
                    wo[:, cf * 1024 + dc2 * 512: cf * 1024 + dc2 * 512 + 512],
                    start=(cf == 0), stop=(cf == DC - 1)), deps=dep)
            ta = pb.emit("vector", lambda e, dc2=dc2: e.tensor_tensor(
                ztile[:, dc2 * 512:dc2 * 512 + 512], psB[:, dc2 * 512:dc2 * 512 + 512],
                ztile[:, dc2 * 512:dc2 * 512 + 512], ALU.add),
                deps=[mm, tz], strict=True)
            pb.tail[("p3_wo%d" % dc2,)] = (ta,)
            st_toks.append(ta)
        tst = pb.dma("sync", cfg["x1"][tc], ztile[:, 0:D], deps=st_toks)
        pb.tail[("p3_z",)] = (tst,)
        x1_toks.append(tst)
    return x1_toks


def emit_retention_p4(pb, bufs, cfg, x1_toks):
    n_tc = cfg["n_tc"]
    gb = emit_gb_bcast(pb, bufs, cfg["ln2g_off"], cfg["ln2b_off"])
    w1, w2 = bufs["w1_bf"], bufs["w2_bf"]
    psH = bufs["ps_s1"]
    psO = bufs["ps_o2b"]
    h1c = bufs["h1c"]
    ztile = bufs["zt"]
    cs = bufs["cstats"]
    xn2 = bufs["xn2t"]
    ptp = bufs["pt"]
    b1v = bufs["b1cols_view"]
    b2r = bufs["b2row"]
    zo_toks = []
    for tc in range(n_tc):
        tz = pb.dma("sync", ztile[:, 0:D], cfg["x1"][tc],
                    deps=list(pb.tail.get(("p4_z",), ())) + list(pb.tail.get(("p3_z",), ()))
                         + [x1_toks[tc]])
        t1 = pb.emit("scalar", lambda e: e.activation(
            bufs["sqscr"][:, 0:D], ztile[:, 0:D], AF.Square, accum_out=cs[:, 1:2]),
            deps=[tz] + list(pb.tail.get(("p4_sq",), ())), strict=True)
        pb.tail[("p4_sq",)] = (t1,)
        t2 = pb.emit("vector", lambda e: e.reduce_sum(cs[:, 0:1], ztile[:, 0:D], axis=AX.X),
                     deps=[tz] + list(pb.tail.get(("p4_cs",), ())), strict=True)
        t3 = pb.emit("vector", lambda e: e.tensor_scalar(cs[:, 2:3], cs[:, 0:1], 1.0 / D, None, ALU.mult, ALU.bypass), deps=[t2], strict=True)
        t4 = pb.emit("vector", lambda e: e.tensor_tensor(cs[:, 3:4], cs[:, 2:3], cs[:, 2:3], ALU.mult), deps=[t3], strict=True)
        t5 = pb.emit("vector", lambda e: e.scalar_tensor_tensor(cs[:, 4:5], cs[:, 1:2], 1.0 / D, cs[:, 3:4], ALU.mult, ALU.subtract), deps=[t4, t1], strict=True)
        t6 = pb.emit("vector", lambda e: e.tensor_scalar(cs[:, 4:5], cs[:, 4:5], EPS_LN, None, ALU.add, ALU.bypass), deps=[t5], strict=True)
        t7 = pb.emit("scalar", lambda e: e.activation(cs[:, 5:6], cs[:, 4:5], AF.Ln, bias=0.0, scale=1.0), deps=[t6], strict=True)
        t8 = pb.emit("scalar", lambda e: e.activation(cs[:, 6:7], cs[:, 5:6], AF.Exp, bias=0.0, scale=-0.5), deps=[t7], strict=True)
        t9 = pb.emit("vector", lambda e: e.tensor_scalar(
            bufs["lnscr"][:, 0:D], ztile[:, 0:D], cs[:, 2:3], cs[:, 6:7],
            ALU.subtract, ALU.mult), deps=[t8] + list(pb.tail.get(("p4_ln",), ())), strict=True)
        t10 = pb.emit("vector", lambda e: e.tensor_tensor(
            bufs["lnscr"][:, 0:D], bufs["lnscr"][:, 0:D], bufs["g_bc"][:, 0:D], ALU.mult),
            deps=[t9] + list(gb), strict=True)
        t11 = pb.emit("vector", lambda e: e.tensor_tensor(
            bufs["tmp_bf"][:, 0:D], bufs["lnscr"][:, 0:D], bufs["b_bc"][:, 0:D], ALU.add),
            deps=[t10] + list(pb.tail.get(("p4_tmp",), ())), strict=True)
        pb.tail[("p4_ln",)] = (t11,)
        tps = []
        for dc in range(DC):
            ph = dc % 2
            tt1 = pb.emit("tensor", lambda e, dc=dc, ph=ph, ptp=ptp: e.transpose(
                ptp[ph][:, 0:128], bufs["tmp_bf"][:, dc * 128:dc * 128 + 128],
                bufs["identb"][:, 0:128]),
                deps=[t11] + list(pb.tail.get(("p4_pt%d" % ph,), ())))
            tt2 = pb.emit("scalar", lambda e, dc=dc, ph=ph, ptp=ptp: e.copy(
                xn2[:, dc * 128:dc * 128 + 128], ptp[ph][:, 0:128]), deps=[tt1],
                strict=True)
            pb.tail[("p4_pt%d" % ph,)] = (tt2,)
            tps.append(tt2)
        pb.tail[("p4_tmp",)] = (tps[-1],)
        xn_tok = tps[-1]
        mm_last = None
        for hf in range(32):
            mm = None
            for dc in range(DC):
                dep = [cfg["w1_tok"], xn_tok] + (list(pb.tail.get(("p4_h",), ())) if dc == 0 else [])
                mm = pb.emit("tensor", lambda e, tc=tc, hf=hf, dc=dc: e.matmul(
                    psH[:, 0:128],
                    w1[:, dc * 4096 + hf * 128: dc * 4096 + hf * 128 + 128],
                    xn2[:, dc * 128: dc * 128 + 128],
                    start=(dc == 0), stop=(dc == DC - 1)), deps=dep)
            th = pb.emit("scalar", lambda e, hf=hf: e.activation(
                h1c[:, 0:128], psH[:, 0:128], AF.Silu,
                bias=b1v[:, hf:hf + 1], scale=1.0),
                deps=[mm, cfg["b1_tok"]] + list(pb.tail.get(("p4_h2",), ())), strict=True)
            pb.tail[("p4_h",)] = (th,)
            wmms = []
            for dc2 in range(2):
                wmm = pb.emit("tensor", lambda e, hf=hf, dc2=dc2: e.matmul(
                    psO[:, dc2 * 512:dc2 * 512 + 512],
                    h1c[:, 0:128],
                    w2[:, hf * 1024 + dc2 * 512: hf * 1024 + dc2 * 512 + 512],
                    start=(hf == 0), stop=False),
                    deps=([th, cfg["w2_tok"]] + (list(pb.tail.get(("p4_o",), ())) if hf == 0 else [])))
                wmms.append(wmm)
            pb.tail[("p4_h2",)] = (wmms[-1],)
            mm_last = wmms[-1]
        x0aps = cfg.get("x0")
        if x0aps is not None:
            x0s = bufs["x0scr"]
            tx0 = pb.dma("gpsimd", x0s[:, 0:D], x0aps[tc],
                         deps=list(pb.tail.get(("p4_x0",), ())))
        ev_toks = []
        for dc2 in range(2):
            bmm = pb.emit("tensor", lambda e, dc2=dc2: e.matmul(
                psO[:, dc2 * 512:dc2 * 512 + 512], bufs["onescol"][0:1, 0:128],
                b2r[0:1, dc2 * 512:dc2 * 512 + 512],
                start=False, stop=True), deps=[mm_last, cfg["b2_tok"]])
            ta = pb.emit("vector", lambda e, dc2=dc2, tc=tc: e.tensor_tensor(
                ztile[:, dc2 * 512:dc2 * 512 + 512],
                psO[:, dc2 * 512:dc2 * 512 + 512],
                ztile[:, dc2 * 512:dc2 * 512 + 512], ALU.add),
                deps=[bmm, tz], strict=True)
            if x0aps is not None:
                ta = pb.emit("vector", lambda e, dc2=dc2, x0s=x0s: e.tensor_tensor(
                    ztile[:, dc2 * 512:dc2 * 512 + 512], x0s[:, dc2 * 512:dc2 * 512 + 512],
                    ztile[:, dc2 * 512:dc2 * 512 + 512], ALU.add),
                    deps=[ta, tx0], strict=True)
            ev_toks.append(ta)
        if x0aps is not None:
            pb.tail[("p4_x0",)] = tuple(ev_toks)
        pb.tail[("p4_o",)] = tuple(ev_toks)
        tst = pb.dma("sync", cfg["zo"][tc], ztile[:, 0:D], deps=ev_toks)
        pb.tail[("p4_z",)] = (tst,)
        zo_toks.append(tst)
    return zo_toks


def build_retention_test(n_tc=32, iters=1, outer_ln=False, final_add=False,
                         it_base=0):
    nc = new_nc()
    es = ExitStack()
    T = n_tc * 128
    H = 8
    zin = nc.dram_tensor("zin", [n_tc, 128, D], F32, kind="ExternalInput").ap()
    x0_d = nc.dram_tensor("x0", [n_tc, 128, D], F32, kind="ExternalInput").ap() if final_add else None
    z0_d = nc.dram_tensor("z0_scr", [n_tc, 128, D], F32).ap() if outer_ln else None
    wq_d = nc.dram_tensor("wq", [iters, H, DC, 128, 128], F32, kind="ExternalInput").ap()
    wk_d = nc.dram_tensor("wk", [iters, H, DC, 128, 128], F32, kind="ExternalInput").ap()
    wv_d = nc.dram_tensor("wv", [iters, H, DC, 128, 128], F32, kind="ExternalInput").ap()
    wg_d = nc.dram_tensor("wg", [iters, DC, 128, D], F32, kind="ExternalInput").ap()
    wo_d = nc.dram_tensor("wo", [iters, DC, 128, D], F32, kind="ExternalInput").ap()
    w1_d = nc.dram_tensor("w1", [iters, DC, 128, 4096], F32, kind="ExternalInput").ap()
    w2_d = nc.dram_tensor("w2", [iters, 32, 128, D], F32, kind="ExternalInput").ap()
    gb_d = nc.dram_tensor("gbrows", [1, iters * 6 * D], BF16, kind="ExternalInput").ap()
    b1_d = nc.dram_tensor("b1cols", [128, iters * 32], F32, kind="ExternalInput").ap()
    b2_d = nc.dram_tensor("b2rows", [1, iters * D], BF16, kind="ExternalInput").ap()
    tabs_d = nc.dram_tensor("xpos", [4, 128, T], BF16, kind="ExternalInput").ap()
    dmask_d = nc.dram_tensor("dmask", [128, H * 128], F32, kind="ExternalInput").ap()
    gcols_d = nc.dram_tensor("gcols", [128, 2 * H], F32, kind="ExternalInput").ap()
    ident = nc.dram_tensor("ident", [128, 128], BF16, kind="ExternalInput").ap()
    ones_d = nc.dram_tensor("onesc", [1, 128], BF16, kind="ExternalInput").ap()
    zout = nc.dram_tensor("zout", [n_tc, 128, D], F32, kind="ExternalOutput").ap()
    yd_d = nc.dram_tensor("yd_scr", [n_tc, 128, D], BF16).ap()
    x1_d = nc.dram_tensor("x1_scr", [n_tc, 128, D], F32).ap()
    z1_d = nc.dram_tensor("z1_scr", [n_tc, 128, D], F32).ap()

    with es:
        pb = PB(nc, es)
        bufs = dict(
            stats=sb(pb, "stats", [128, 4 * n_tc], F32),
            tmp_bf=sb(pb, "tmp_bf", [128, D], BF16),
            identb=sb(pb, "identb", [128, 128], BF16),
            onescol=sb(pb, "onescol", [1, 128], BF16),
            gbrows=sb(pb, "gbrows_b", [1, 6 * D], BF16),
            b1cols=sb(pb, "b1cols_b", [128, iters * 32], F32),
            b2rows=sb(pb, "b2rows_b", [1, iters * D], BF16),
            g_bc=sb(pb, "g_bc", [128, D], F32),
            b_bc=sb(pb, "b_bc", [128, D], F32),
            gnw_bc=sb(pb, "gnw_bc", [128, H * 128], F32),
            gnb_bc=sb(pb, "gnb_bc", [128, H * 128], F32),
            lnscr=sb(pb, "lnscr", [128, D], F32),
            dram_scr=sb(pb, "dram_scr", [128, 2 * D], F32),
            zt=sb(pb, "zt", [128, D], F32),
            wstage=sb(pb, "wstage", [128, 1024], F32),
            sqscr=sb(pb, "sqscr", [128, D], F32),
            cstats=sb(pb, "cstats", [128, 8], F32),
            gcol=sb(pb, "gcol", [128, H], F32),
            ghcol=sb(pb, "ghcol", [128, H], F32),
            dmask=sb(pb, "dmask_b", [128, H * 128], F32),
        )
        c1 = pb.dma("sync", bufs["identb"][:, :], ident)
        c2 = pb.dma("sync", bufs["onescol"][0:1, :], ones_d)
        c3 = pb.dma("sync", bufs["gbrows"][0:1, :], gb_d[0:1, 0:6 * D])
        c4 = pb.dma("sync", bufs["b1cols"][:, :], b1_d)
        c5 = pb.dma("sync", bufs["b2rows"][0:1, :], b2_d)
        c6 = pb.dma("sync", bufs["gcol"][:, :], gcols_d[:, 0:H])
        c7 = pb.dma("sync", bufs["ghcol"][:, :], gcols_d[:, H:2 * H])
        c8 = pb.dma("sync", bufs["dmask"][:, :], dmask_d)
        bufs["gb_tok"] = c3
        const_tok = c8
        gammas = 1.0 - np.exp(np.linspace(np.log(1.0 / 32), np.log(1.0 / 512), H))
        entry = [c1, c2, c3, c4, c5, c6, c7, c8]
        if outer_ln:
            es0 = ExitStack()
            with es0:
                sv0 = pb.es; pb.es = es0
                bufs["lnt0"] = sb(pb, "lnt0", [128, D], F32)
                bufs["lnt1"] = sb(pb, "lnt1", [128, D], F32)
                pb.es = sv0
                ln0 = emit_ln_stream(pb, bufs, [zin[t] for t in range(n_tc)],
                                     [z0_d[t] for t in range(n_tc)], n_tc, tok0=entry)
            del bufs["lnt0"], bufs["lnt1"]
            entry = entry + [ln0[-1]]
            zi_aps = [z0_d[t] for t in range(n_tc)]
        else:
            zi_aps = [zin[t] for t in range(n_tc)]
        for it in range(iters):
            if it > 0:
                c3 = pb.dma("sync", bufs["gbrows"][0:1, :],
                            gb_d[0:1, it * 6 * D:(it + 1) * 6 * D], deps=entry)
                bufs["gb_tok"] = c3
            zo_aps = [zout[t] for t in range(n_tc)] if it == iters - 1 else [z1_d[t] for t in range(n_tc)]
            cfg = dict(
                n_tc=n_tc, iter=it,
                zi=zi_aps, zo=zo_aps,
                yd=[yd_d[t] for t in range(n_tc)],
                x1=[x1_d[t] for t in range(n_tc)],
                wq=[[wq_d[it, h, c] for c in range(DC)] for h in range(H)],
                wk=[[wk_d[it, h, c] for c in range(DC)] for h in range(H)],
                wv=[[wv_d[it, h, c] for c in range(DC)] for h in range(H)],
                ln1g_off=0 * D, ln1b_off=1 * D,
                ln2g_off=2 * D, ln2b_off=3 * D,
                gng_off=4 * D, gnb_off=5 * D,
                gC=[float(g) ** 128 for g in gammas],
                dk_tok=const_tok, tab_tok=None,
                b1_tok=c4, b2_tok=c5,
                entry=list(entry),
            )
            es1 = ExitStack()
            with es1:
                sv = pb.es; pb.es = es1
                bufs["pt"] = (ps(pb, "ptP1a%d" % it, [128, 128], BF16),
                              ps(pb, "ptP1b%d" % it, [128, 128], BF16))
                bufs["ps_bc"] = ps(pb, "psbc1%d" % it, [128, 512])
                pb.es = sv
                gn_toks = []
                prevgn = list(pb.tail.get(("gb_bc",), ())) + [pb.barrier_tok("vector")]
                for h in range(H):
                    for which, dstt in ((cfg["gng_off"], bufs["gnw_bc"]),
                                        (cfg["gnb_off"], bufs["gnb_bc"])):
                        psbcl = bufs["ps_bc"]
                        tm1 = pb.emit("tensor", lambda e, which=which, h=h, psbcl=psbcl: e.matmul(
                            psbcl[:, 0:128], bufs["onescol"][0:1, 0:128],
                            bufs["gbrows"][0:1, which + h * 128: which + h * 128 + 128],
                            start=True, stop=True), deps=[bufs["gb_tok"]] + prevgn)
                        tcp = pb.emit("scalar", lambda e, dstt=dstt, h=h, psbcl=psbcl: e.copy(
                            dstt[:, h * 128:(h + 1) * 128], psbcl[:, 0:128]),
                            deps=[tm1])
                        prevgn = [tcp]
                        gn_toks.append(tcp)
                pb.tail[("gb_bc",)] = (gn_toks[-1],)
                cfg["gn_tok"] = gn_toks[-1]
                gb = emit_gb_bcast(pb, bufs, cfg["ln1g_off"], cfg["ln1b_off"])
                esKV = ExitStack()
                svv = pb.es; pb.es = esKV
                xkv = sb(pb, "xkv%d" % it, [128, DC * T], BF16)
                pb.es = svv
                bufs["xkv_fm"] = xkv
                ln1 = emit_ln_fm(pb, None, xkv, bufs["stats"], bufs["tmp_bf"][:, 0:D],
                                 bufs["tmp_bf"], bufs["identb"], bufs["pt"], n_tc=n_tc,
                                 tok_x=list(gb) + list(entry),
                                 affine_wb=(bufs["g_bc"], bufs["b_bc"], bufs["lnscr"]),
                                 src_dram=zi_aps, dram_scr=bufs["dram_scr"], dst_stride=T)
            es2 = ExitStack()
            with es2:
                sv = pb.es; pb.es = es2
                for nm, shp, dt_ in (("q_fm", [128, T], BF16), ("k_fm", [128, T], BF16),
                                     ("qrot", [128, T], BF16), ("krot", [128, T], BF16),
                                     ("khat", [128, T], BF16), ("v_tm", [128, T], BF16),
                                     ("tscr1", [128, T], BF16),
                                     ("whb", [128, 3 * 1024], BF16),
                                     ("abuf", [128, 128], BF16),
                                     ("obuf", [128, 128], F32),
                                     ("obuf2", [128, 128], BF16),
                                     ("state", [128, 128], F32),
                                     ("state_bf", [128, 128], BF16),
                                     ("tab", [128, 2 * T], BF16)):
                    bufs[nm] = sb(pb, nm + "_%d" % it, shp, dt_)
                bufs["ps_s0"] = ps(pb, "psA%d" % it, [128, 512])
                bufs["ps_s1"] = ps(pb, "psB%d" % it, [128, 512])
                bufs["ps_o1"] = ps(pb, "psO1%d" % it, [128, 128])
                bufs["ps_o2"] = ps(pb, "psO2%d" % it, [128, 128])
                bufs["ps_su"] = ps(pb, "psSU%d" % it, [128, 128])
                bufs["ptB"] = (ps(pb, "ptB0%d" % it, [128, 128], BF16),
                               ps(pb, "ptB1%d" % it, [128, 128], BF16))
                pb.es = sv
                cfg["tabs_d"] = [tabs_d[j] for j in range(4)]
                _, yd_toks = None, None
                ln1_, yd_toks = emit_retention_iter_p2(pb, bufs, cfg, ln1)
            es3 = ExitStack()
            with es3:
                sv = pb.es; pb.es = es3
                bufs["wg_bf"] = sb(pb, "wg_bf%d" % it, [128, DC * D], BF16)
                bufs["wo_bf"] = sb(pb, "wo_bf%d" % it, [128, DC * D], BF16)
                bufs["ytile"] = sb(pb, "ytile%d" % it, [128, D], BF16)
                bufs["gtile"] = sb(pb, "gtile%d" % it, [128, 512], BF16)
                bufs["yfm"] = sb(pb, "yfm%d" % it, [128, D], BF16)
                bufs["ps_s0"] = ps(pb, "psA3%d" % it, [128, 512])
                bufs["ps_s1"] = ps(pb, "psB3%d" % it, [128, 1024])
                bufs["ptB"] = (ps(pb, "ptB30%d" % it, [128, 128], BF16),
                               ps(pb, "ptB31%d" % it, [128, 128], BF16))
                pb.es = sv
                _, cfg["wg_tok"] = emit_weight_load(pb, [wg_d[it, c] for c in range(DC)],
                                                    bufs["wstage"], bufs["wg_bf"], DC, 1024,
                                                    deps=[yd_toks[-1]], eng_dma="sync")
                _, cfg["wo_tok"] = emit_weight_load(pb, [wo_d[it, c] for c in range(DC)],
                                                    bufs["wstage"], bufs["wo_bf"], DC, 1024,
                                                    deps=[], eng_dma="sync")
                x1_toks = emit_retention_p3(pb, bufs, cfg, yd_toks, ln1)
            esKV.close()
            es4 = ExitStack()
            with es4:
                sv = pb.es; pb.es = es4
                if final_add and it == iters - 1:
                    bufs["x0scr"] = sb(pb, "x0scr_%d" % it, [128, D], F32)
                    cfg["x0"] = [x0_d[t] for t in range(n_tc)]
                bufs["xn2t"] = sb(pb, "xn2t_%d" % it, [128, D], BF16)
                bufs["w1_bf"] = sb(pb, "w1_bf%d" % it, [128, DC * 4096], BF16)
                bufs["w2_bf"] = sb(pb, "w2_bf%d" % it, [128, 32 * 1024], BF16)
                bufs["h1c"] = sb(pb, "h1c%d" % it, [128, 128], BF16)
                bufs["ps_s1"] = ps(pb, "psH4%d" % it, [128, 512])
                bufs["ps_bc"] = ps(pb, "psbc4%d" % it, [128, 512])
                bufs["pt"] = (ps(pb, "ptP4a%d" % it, [128, 128], BF16),
                              ps(pb, "ptP4b%d" % it, [128, 128], BF16))
                bufs["ps_o2b"] = ps(pb, "psO4%d" % it, [128, 1024])
                pb.es = sv
                cfg["b1_tok"] = c4
                bufs["b1cols_it"] = bufs["b1cols"]
                _, cfg["w1_tok"] = emit_weight_load(
                    pb, [w1_d[it, c // 4][:, (c % 4) * 1024:(c % 4) * 1024 + 1024]
                         for c in range(DC * 4)],
                    bufs["wstage"], bufs["w1_bf"], DC * 4, 1024,
                    deps=[x1_toks[-1]], eng_dma="gpsimd")
                _, cfg["w2_tok"] = emit_weight_load(
                    pb, [w2_d[it, c] for c in range(32)],
                    bufs["wstage"], bufs["w2_bf"], 32, 1024,
                    deps=[], eng_dma="gpsimd")
                cfg["b1_base"] = it * 32
                bufs["b1cols_view"] = bufs["b1cols"][:, it * 32:(it + 1) * 32]
                bufs["b2row"] = bufs["b2rows"][0:1, it * D:(it + 1) * D]
                zo_toks = emit_retention_p4(pb, bufs, cfg, x1_toks)
            zi_aps = zo_aps
            entry = pb.all_toks()
        pb.emit("sync", lambda e: e.nop(), deps=zo_toks)
        pb.build()
    return nc



_PROG_CACHE = {}

def _prog(name, fn):
    if name not in _PROG_CACHE:
        _PROG_CACHE[name] = fn()
    return _PROG_CACHE[name]


def _halves(x):
    out = []
    for c in range(8):
        b, h = c // 2, c % 2
        out.append(np.ascontiguousarray(x[b, h * 2048:(h + 1) * 2048].reshape(16, 128, 1024)))
    return out


def _unhalves(res, key):
    x = np.empty((4, 4096, 1024), np.float32)
    for c in range(8):
        b, h = c // 2, c % 2
        x[b, h * 2048:(h + 1) * 2048] = np.asarray(res[c][key]).reshape(2048, 1024)
    return x


def kernel(**z):
    import math
    ONES = np.ones((1, 128), BF)
    B = 4
    x = np.asarray(z["L0_attn_a_X"], np.float32)

    H8, HD = 8, 128
    perm = np.concatenate([np.arange(0, HD, 2), np.arange(1, HD, 2)])
    WQ, WK, WV = z["L0_attn_a_WQ"], z["L0_attn_a_WK"], z["L0_attn_a_WV"]
    T4 = 4096
    sv = (np.arange(0, HD, 2) + 0.4 * HD) / (1.4 * HD)
    power = (np.arange(T4, dtype=np.float64) / 512)[:, None]
    scale = sv[None, :] ** power
    inv_freq = 1.0 / 10000.0 ** (np.arange(64, dtype=np.float64) / 64)
    ang = np.arange(T4, dtype=np.float64)[:, None] * inv_freq[None, :]
    sin_, cos_ = np.sin(ang), np.cos(ang)
    tabs = np.stack([np.ascontiguousarray(
        np.concatenate([a.T, a.T], 0).astype(BF))
        for a in (cos_ * scale, sin_ * scale, cos_ / scale, sin_ / scale)])
    gammas = 1.0 - np.exp(np.linspace(np.log(1 / 32), np.log(1 / 512), H8))
    pp = np.arange(128)
    dmask = np.concatenate([np.where(pp[None, :] >= pp[:, None],
                                     g ** (pp[None, :] - pp[:, None]).astype(np.float64), 0.0).astype(np.float32)
                            for g in gammas], 1)
    gcols = np.concatenate([np.stack([g ** (pp + 1) for g in gammas], 1),
                            np.stack([g ** (127 - pp) for g in gammas], 1)], 1).astype(np.float32)
    gbrows = np.concatenate([np.concatenate(
        [z["L0_attn_a_ln1_w"][i], z["L0_attn_a_ln1_b"][i],
         z["L0_attn_a_ln2_w"][i], z["L0_attn_a_ln2_b"][i],
         z["L0_attn_a_gn_w"][i], z["L0_attn_a_gn_b"][i]]) for i in range(2)]
        ).astype(BF)[None, :]
    imr = dict(
        wq=np.ascontiguousarray(WQ[:, :, :, perm].reshape(2, H8, 8, 128, 128)),
        wk=np.ascontiguousarray(WK[:, :, :, perm].reshape(2, H8, 8, 128, 128)),
        wv=np.ascontiguousarray(WV.reshape(2, H8, 8, 128, 128)),
        wg=np.ascontiguousarray(z["L0_attn_a_WG"].reshape(2, 8, 128, D)),
        wo=np.ascontiguousarray(z["L0_attn_a_WO"].reshape(2, 8, 128, D)),
        w1=np.ascontiguousarray(z["L0_attn_a_ffn_w1"].reshape(2, 8, 128, 4096)),
        w2=np.ascontiguousarray(z["L0_attn_a_ffn_w2"].reshape(2, 32, 128, D)),
        gbrows=gbrows,
        b1cols=np.ascontiguousarray(np.concatenate(
            [z["L0_attn_a_ffn_b1"][i].reshape(32, 128).T for i in range(2)], 1)),
        b2rows=z["L0_attn_a_ffn_b2"].ravel().astype(BF)[None, :],
        xpos=tabs, dmask=dmask, gcols=gcols,
        ident=IDENT_BF, onesc=ONES)
    RET = _prog("RET", lambda: build_retention_test(n_tc=32, iters=2,
                                                    outer_ln=True, final_add=True))
    in_maps = []
    for c in range(8):
        b = c // 2
        im = dict(imr)
        im["zin"] = np.ascontiguousarray(x[b].reshape(32, 128, 1024))
        im["x0"] = im["zin"]
        in_maps.append(im)
    res = run_bass_kernel_spmd(RET, in_maps, core_ids=list(range(8)))
    xcur = np.stack([np.asarray(res.results[2 * b]["zout"]).reshape(4096, 1024)
                     for b in range(B)])

    ML0 = _prog("ML0", lambda: build_moe_l0_test(n_tc=16))
    im_m = dict(
        gwT=np.ascontiguousarray(z["L0_ffn_gate_w"].T.reshape(8, 128, 8)
                                 .transpose(1, 0, 2).reshape(128, 64)),
        gu=np.ascontiguousarray(z["L0_ffn_gate_up_proj"].reshape(8, 8, 128, 2048)),
        down=np.ascontiguousarray(z["L0_ffn_down_proj"].reshape(8, 8, 128, 1024)),
        ident=IDENT_BF)

    def run_moe_l0(xc):
        hs = _halves(xc)
        ims = [dict(im_m, xin=hs[c]) for c in range(8)]
        r = run_bass_kernel_spmd(ML0, ims, core_ids=list(range(8)))
        return _unhalves(r.results, "xout")

    ppp, fff = np.arange(128)[:, None], np.arange(512)[None, :]
    cmask = np.concatenate([np.where(ppp + idx * 128 > fff, -1e30, 0).astype(np.float32)
                            for idx in range(4)], 1)

    def attn_im_common(style):
        im = dict(ident=IDENT_BF, onesc=ONES, cmask=cmask)
        if style == "tanh":
            Wq, Wk, Wv, Wo = z["L1_attn_a_Wq"], z["L1_attn_a_Wk"], z["L1_attn_a_Wv"], z["L1_attn_a_Wo"]
            bq, bk, bv, bo = z["L1_attn_a_bq"], z["L1_attn_a_bk"], z["L1_attn_a_bv"], z["L1_attn_a_bo"]
            bqk = np.zeros((64, 32), np.float32)
            for h in range(16):
                bqk[:, 2 * h] = bq[h * 64:(h + 1) * 64]
                bqk[:, 2 * h + 1] = bk[h * 64:(h + 1) * 64]
            im["bqk"] = bqk
            im["bvo"] = np.concatenate([bv, bo]).astype(BF)[None, :]
        else:
            Wq = np.ascontiguousarray(z["L0_attn_b_Wq"].T)
            Wk = np.ascontiguousarray(z["L0_attn_b_Wk"].T)
            Wv = np.ascontiguousarray(z["L0_attn_b_Wv"].T)
            Wo = np.ascontiguousarray(z["L0_attn_b_Wp"].T)
            for n2 in "qkv":
                im["bT" + n2] = np.ascontiguousarray(z["L0_attn_b_B" + n2].T.reshape(8, 128, 8))
                im["aT" + n2] = np.ascontiguousarray(z["L0_attn_b_A" + n2].T)
        im["wq"] = np.ascontiguousarray(Wq.reshape(8, 128, 16, 64).transpose(2, 0, 1, 3))
        im["wk"] = np.ascontiguousarray(Wk.reshape(8, 128, 16, 64).transpose(2, 0, 1, 3))
        im["wv"] = np.ascontiguousarray(Wv.reshape(8, 128, 16, 64).transpose(2, 0, 1, 3))
        im["wo"] = np.ascontiguousarray(Wo.reshape(8, 128, 1024))
        return im

    AT = _prog("AT", lambda: build_attn_test(nq_tc=16, style="tanh"))
    AA = _prog("AA", lambda: build_attn_test(nq_tc=16, style="alibi"))

    def run_attn(prog, xc, style):
        imc = attn_im_common(style)
        zeros_h = np.zeros((16, 128, 1024), np.float32)
        ims = []
        for c in range(8):
            b, h = c // 2, c % 2
            im = dict(imc)
            im["xin"] = np.ascontiguousarray(xc[b, h * 2048:(h + 1) * 2048].reshape(16, 128, 1024))
            im["xprev"] = (np.ascontiguousarray(xc[b, 0:2048].reshape(16, 128, 1024))
                           if h == 1 else zeros_h)
            km = np.zeros((128, 32), np.float32)
            if h == 0:
                km[:, 0:16] = -1e30
            im["kmask"] = km
            base = h * 2048 - 2048
            im["kpos"] = (base + np.arange(32)[None, :] * 128
                          + np.arange(128)[:, None]).astype(np.float32)
            ims.append(im)
        r = run_bass_kernel_spmd(prog, ims, core_ids=list(range(8)))
        return _unhalves(r.results, "xout")

    ML1 = _prog("ML1", lambda: build_moe_l1_test(n_tc=16))
    im_f = dict(
        gwT=np.ascontiguousarray(z["L1_ffn_router_w"].reshape(8, 128, 8)
                                 .transpose(1, 0, 2).reshape(128, 64)),
        ebias=np.ascontiguousarray(np.broadcast_to(z["L1_ffn_expert_bias"], (128, 8))),
        gu=np.ascontiguousarray(np.concatenate([z["L1_ffn_w1"], z["L1_ffn_w3"]], axis=2)
                                .reshape(8, 8, 128, 2048)),
        down=np.ascontiguousarray(z["L1_ffn_w2"].reshape(8, 8, 128, 1024)),
        sgu=np.ascontiguousarray(np.concatenate([z["L1_ffn_sw1"], z["L1_ffn_sw3"]], axis=1)
                                 .reshape(8, 128, 2048)),
        sdown=np.ascontiguousarray(z["L1_ffn_sw2"].reshape(8, 128, 1024)),
        ident=IDENT_BF)

    def run_moe_l1(xc):
        hs = _halves(xc)
        ims = [dict(im_f, xin=hs[c]) for c in range(8)]
        r = run_bass_kernel_spmd(ML1, ims, core_ids=list(range(8)))
        return _unhalves(r.results, "xout")

    MG = _prog("MG", lambda: build_mingru_test(n_tc=16))
    im_g = dict(
        w=np.ascontiguousarray(z["L1_attn_b_W"].reshape(3, 8, 128, 2048)),
        gbrows=np.concatenate([z["L1_attn_b_ln_gamma"].ravel(),
                               z["L1_attn_b_ln_beta"].ravel()]).astype(BF)[None, :],
        bcols=np.ascontiguousarray(z["L1_attn_b_b"].reshape(3, 16, 128)
                                   .transpose(2, 0, 1).reshape(128, 48)),
        ident=IDENT_BF, identf_d=np.eye(128, dtype=np.float32), onesc=ONES)

    def run_mingru(xc):
        ims = []
        for c in range(8):
            b = c // 2
            im = dict(im_g, xin=np.ascontiguousarray(xc[b, 0:2048].reshape(16, 128, 1024)),
                      initc=np.full((128, 24), 0.5, np.float32))
            ims.append(im)
        r1 = run_bass_kernel_spmd(MG, ims, core_ids=list(range(8)))
        ims2 = []
        for c in range(8):
            b = c // 2
            im = dict(im_g, xin=np.ascontiguousarray(xc[b, 2048:4096].reshape(16, 128, 1024)),
                      initc=np.asarray(r1.results[2 * b]["hend"]))
            ims2.append(im)
        r2 = run_bass_kernel_spmd(MG, ims2, core_ids=list(range(8)))
        out = np.empty((4, 4096, 1024), np.float32)
        for b in range(B):
            out[b, 0:2048] = np.asarray(r1.results[2 * b]["xout"]).reshape(2048, 1024)
            out[b, 2048:4096] = np.asarray(r2.results[2 * b]["xout"]).reshape(2048, 1024)
        return out

    xcur = run_moe_l0(xcur)
    xcur = run_attn(AA, xcur, "alibi")
    xcur = run_moe_l0(xcur)
    xcur = run_attn(AT, xcur, "tanh")
    xcur = run_moe_l1(xcur)
    xcur = run_mingru(xcur)
    xcur = run_moe_l1(xcur)
    return xcur.astype(np.float32)
